# version 05 (final)
# baseline (speedup 1.0000x reference)
import numpy as np
import ml_dtypes
from contextlib import ExitStack

import concourse.bass as bass
import concourse.mybir as mybir
from concourse.bass_utils import run_bass_kernel_spmd

F32 = mybir.dt.float32
BF16 = mybir.dt.bfloat16
U8 = mybir.dt.uint8
AF = mybir.ActivationFunctionType
ALU = mybir.AluOpType
AX = mybir.AxisListType
BF = ml_dtypes.bfloat16

NCORE = 8
NTOK = 16384
T = 2048
D = 1024
E = 8
EHE = 2048
SHH = 4096
CAP = 5120
NT = T // 128
NR = NTOK // 128
NITER = 50
NEG = -1.0e30


class Sem:
    def __init__(self, stack, nc, name):
        self.h = stack.enter_context(nc.semaphore(name))
        self.n = 0

    def inc(self, inst, k=1):
        inst.then_inc(self.h, k)
        self.n += k
        return self.n


def build_program(debug=(), stop_after=None, cstop=None, b_units=None, kc_rev=False,
                  fc2_serial=False, fc2_nh_outer=False, slot_shift=False,
                  fc2_dup0=False):
    dbg = set(debug)
    nc = bass.Bass("TRN2", target_bir_lowering=False)

    din = {}

    def I(name, shape, dt_=F32):
        din[name] = nc.dram_tensor(name, shape, dt_, kind="ExternalInput").ap()
        return din[name]

    d_xL = I("xL", [T, D])
    d_xT = I("xT", [D, NTOK])
    d_wqT = I("wqT", [D, 1024])
    d_wkTd = I("wkTd", [D, 2048])
    d_wvTd = I("wvTd", [D, 2048])
    d_wpT = I("wpT64", [64, 16 * 1024])
    d_bprojB = I("bprojB", [128, D])
    d_fc1T = I("fc1T", [E, D, EHE])
    d_fc2T = I("fc2T", [E, EHE, D])
    d_sh1T = I("sh1T", [D, SHH])
    d_sh2T = I("sh2T", [SHH, D])
    d_fc1bP = I("fc1bP", [128, E * 16])
    d_sh1bP = I("sh1bP", [128, 32])
    d_fc2bA = I("fc2bA", [E, D])
    d_sh2bB = I("sh2bB", [128, D])
    d_pw9 = I("pw9", [D, 9])
    d_ebP = I("ebP", [128, E])
    d_pbP = I("pbP", [128, E])
    d_cid = I("cid", [128, 1])
    d_bisc = I("bisc", [128, 41])
    d_epsB = I("epsB", [128, 1])
    d_cio16 = I("cio16", [128, NR])
    d_ones = I("cones", [128, 128])
    d_lt = I("clt", [128, 128])
    d_ident = I("cident", [128, 128])
    d_bdmask = I("cbdmask", [128, 128])
    d_bdneg = I("cbdneg", [128, 128])
    d_iotaE = I("ciotaE", [128, E * NR])
    d_e8 = I("ce8", [8, E * 128])

    d_y = nc.dram_tensor("y", [T, D], F32, kind="ExternalOutput").ap()
    d_gsp = nc.dram_tensor("gspill", [T, D], BF16, kind="Internal").ap()

    d_dbg = {}
    for name, shape in (("z", [128, E * NR]), ("w1", [128, NR]), ("w2", [128, NR]),
                        ("e1", [128, NR]), ("e2", [128, NR]), ("V", [128, E]),
                        ("r", [128, E]), ("g1", [128, NR]), ("g2", [128, NR]),
                        ("gates", [128, E * NT]), ("W", [128, E * NR]),
                        ("m1", [128, NR]), ("m2", [128, NR]),
                        ("acc2", [128, NT * D]), ("hbuf", [128, 16 * T]),
                        ("wf2", [128, 16 * D]),
                        ("xnT", [128, 8 * T]), ("svT", [64, 16 * 512]),
                        ("qBD", [128, 16 * 512]), ("kBD", [128, 16 * 512])):
        if name in dbg:
            d_dbg[name] = nc.dram_tensor("dbg_" + name, shape, F32,
                                         kind="ExternalOutput").ap()

    stack = ExitStack()
    sb = lambda n, s, dt_, **kw: stack.enter_context(nc.sbuf_tensor(n, s, dt_, **kw))
    ps = lambda n, s, dt_: stack.enter_context(nc.psum_tensor(n, s, dt_))

    psA = ps("psA", [128, 512], F32)
    psB = ps("psB", [128, 512], F32)
    psC = ps("psC", [128, 1024], F32)
    psD = ps("psD", [128, 1024], F32)
    psT = ps("psT", [128, 512], BF16)
    psU = ps("psU", [128, 512], BF16)

    xnT = sb("xnT", [128, 8 * T], BF16)
    gates = sb("gates", [128, E * NT], F32)
    c_ones = sb("c_ones", [128, 128], BF16)
    c_lt = sb("c_lt", [128, 128], BF16)
    c_ident = sb("c_ident", [128, 128], BF16)
    c_e8 = sb("c_e8", [8, E * 128], BF16)
    w_fc1bP = sb("w_fc1bP", [128, E * 16], F32)
    w_sh1bP = sb("w_sh1bP", [128, 32], F32)
    w_fc2bA = sb("w_fc2bA", [8, D], BF16)
    w_pw9 = sb("w_pw9", [128, 8 * 9], BF16)
    w_ebP = sb("w_ebP", [128, E], F32)
    w_pbP = sb("w_pbP", [128, E], F32)
    w_cid = sb("w_cid", [128, 1], F32)
    w_bisc = sb("w_bisc", [128, 41], F32)
    w_epsB = sb("w_epsB", [128, 1], F32)
    c_io16 = sb("c_io16", [128, NR], F32)

    QS = Sem(stack, nc, "qs")
    QG = Sem(stack, nc, "qg")
    TS = Sem(stack, nc, "ts")
    VS = Sem(stack, nc, "vs")
    SS = Sem(stack, nc, "ss")

    block = stack.enter_context(nc.Block())

    ascope = ExitStack()
    asb = lambda n, s, dt_, **kw: ascope.enter_context(nc.sbuf_tensor(n, s, dt_, **kw))
    ln_x = asb("ln_x", [128, 2 * D], F32)
    ln_sq = asb("ln_sq", [128, D], F32)
    ln_st = asb("ln_st", [128, 24], F32)
    ln_xn = asb("ln_xn", [128, 2 * D], BF16)
    xtc = asb("xtc", [128, 2 * 8 * 1024], BF16)
    xsq = asb("xsq", [128, 8 * 1024], BF16)
    zraw = asb("zraw", [128, 16 * 80], F32)
    r_z = asb("r_z", [128, E * NR], F32)
    r_a = asb("r_a", [128, E * NR], F32)
    r_b = asb("r_b", [128, E * NR], F32)
    r_eqb = asb("r_eqb", [128, E * NR], BF16)
    r_u8 = asb("r_u8", [128, E * NR], U8)
    r_io = asb("r_io", [128, E * NR], F32)
    r_m1 = asb("r_m1", [128, NR], F32)
    r_m2 = asb("r_m2", [128, NR], F32)
    r_e1 = asb("r_e1", [128, NR], F32)
    r_e2 = asb("r_e2", [128, NR], F32)
    r_w1 = asb("r_w1", [128, NR], F32)
    r_w2 = asb("r_w2", [128, NR], F32)
    r_t1 = asb("r_t1", [128, NR], F32)
    r_t2 = asb("r_t2", [128, NR], F32)
    r_t3 = asb("r_t3", [128, NR], F32)
    r_g1 = asb("r_g1", [128, NR], F32)
    r_g2 = asb("r_g2", [128, NR], F32)
    r_sw = asb("r_sw", [128, E], F32)
    r_midt = asb("r_midt", [128, E], F32)
    r_loS = asb("r_loS", [128, E], F32)
    r_hiS = asb("r_hiS", [128, E], F32)
    r_rrS = asb("r_rrS", [128, E], F32)
    r_cnt = asb("r_cnt", [128, E], BF16)
    r_bit = asb("r_bit", [128, E], BF16)
    r_ve = asb("r_ve", [128, NR], F32)
    r_re = asb("r_re", [128, NR], F32)
    r_rk = asb("r_rk", [128, NR], F32)
    r_ga = asb("r_ga", [128, E * NR], F32)

    @block.sync
    def _(sync):
        for dst, src in ((w_fc1bP[:], d_fc1bP[:]), (w_sh1bP[:], d_sh1bP[:]),
                         (w_ebP[:], d_ebP[:]), (w_pbP[:], d_pbP[:]),
                         (w_cid[:], d_cid[:]), (w_bisc[:], d_bisc[:]),
                         (w_epsB[:], d_epsB[:]),
                         (c_io16[:], d_cio16[:]),
                         (r_io[:], d_iotaE[:])):
            QS.inc(sync.dma_start(out=dst, in_=src), 16)
    m_sconsts = QS.n

    @block.gpsimd
    def _(g):
        for dst, src in ((c_ones[:], d_ones[:]), (c_lt[:], d_lt[:]),
                         (c_ident[:], d_ident[:]), (c_e8[:], d_e8[:]),
                         (w_fc2bA[:], d_fc2bA[:]),
                         (w_pw9[:].rearrange("p (c n) -> p c n", c=8),
                          d_pw9.rearrange("(c p) n -> p c n", p=128))):
            QG.inc(g.dma_start(out=dst, in_=src), 16)
    m_gconsts = QG.n

    qs0, ss0, vs0, ts0 = QS.n, SS.n, VS.n, TS.n
    xLr = d_xL.rearrange("(n p) d -> n p d", p=128)

    @block.sync
    def _(sync):
        for i in range(NT):
            if i >= 2:
                sync.wait_ge(SS.h, ss0 + 3 * (i - 2) + 3)
            QS.inc(sync.dma_start(out=ln_x[:, (i % 2) * D:(i % 2 + 1) * D],
                                  in_=xLr[i]), 16)
    assert QS.n == qs0 + 16 * NT

    @block.scalar
    def _(s):
        for i in range(NT):
            r = i % 2
            b = 12 * r
            muraw = ln_st[:, b + 0:b + 1]
            ssc = ln_st[:, b + 1:b + 2]
            muc = ln_st[:, b + 2:b + 3]
            tA = ln_st[:, b + 3:b + 4]
            tB = ln_st[:, b + 4:b + 5]
            var = ln_st[:, b + 5:b + 6]
            vare = ln_st[:, b + 6:b + 7]
            std = ln_st[:, b + 7:b + 8]
            rstd = ln_st[:, b + 8:b + 9]
            t2 = ln_st[:, b + 9:b + 10]
            nmr = ln_st[:, b + 10:b + 11]
            s.wait_ge(QS.h, qs0 + 16 * (i + 1))
            s.activation(ln_sq[:], ln_x[:, r * D:(r + 1) * D], AF.Square,
                         accum_out=ssc)
            SS.inc(s.drain())
            s.wait_ge(VS.h, vs0 + 3 * i + 1)
            s.activation(muc, muraw, AF.Identity, scale=1.0 / D)
            s.drain()
            s.activation(tA, muc, AF.Square)
            s.drain()
            s.activation(tB, tA, AF.Identity, scale=-1.0)
            s.drain()
            s.activation(var, ssc, AF.Identity, scale=1.0 / D, bias=tB)
            s.drain()
            s.activation(vare, var, AF.Identity, bias=w_epsB[:, 0:1])
            s.drain()
            SS.inc(s.activation(std, vare, AF.Sqrt))
            s.wait_ge(VS.h, vs0 + 3 * i + 2)
            s.activation(t2, muc, AF.Identity, scale=rstd)
            s.drain()
            s.activation(nmr, t2, AF.Identity, scale=-1.0)
            s.drain()
            SS.inc(s.activation(ln_xn[:, r * D:(r + 1) * D],
                                ln_x[:, r * D:(r + 1) * D], AF.Identity,
                                scale=rstd, bias=nmr))
    assert SS.n == ss0 + 3 * NT

    @block.vector
    def _(v):
        for i in range(NT):
            r = i % 2
            b = 12 * r
            v.wait_ge(QS.h, qs0 + 16 * (i + 1))
            VS.inc(v.tensor_reduce(ln_st[:, b + 0:b + 1],
                                   ln_x[:, r * D:(r + 1) * D], AX.X, ALU.add))
            v.wait_ge(SS.h, ss0 + 3 * i + 2)
            VS.inc(v.reciprocal(ln_st[:, b + 8:b + 9], ln_st[:, b + 7:b + 8]))
            v.wait_ge(TS.h, ts0 + i + 1)
            for c in range(8):
                srcp = (psT[:, (c % 4) * 128:(c % 4 + 1) * 128] if c < 4
                        else psU[:, (c - 4) * 128:(c - 3) * 128])
                ev = v.tensor_copy(xnT[:, c * T + i * 128:c * T + (i + 1) * 128], srcp)
            VS.inc(ev)
    assert VS.n == vs0 + 3 * NT

    @block.tensor
    def _(t):
        for i in range(NT):
            r = i % 2
            t.wait_ge(SS.h, ss0 + 3 * i + 3)
            if i >= 1:
                t.wait_ge(VS.h, vs0 + 3 * (i - 1) + 3)
            for c in range(8):
                dst = (psT[:, (c % 4) * 128:(c % 4 + 1) * 128] if c < 4
                       else psU[:, (c - 4) * 128:(c - 3) * 128])
                mm = t.transpose(dst, ln_xn[:, r * D + c * 128:r * D + (c + 1) * 128],
                                 c_ident[:])
            TS.inc(mm)
    assert TS.n == ts0 + NT
    m_xnT = VS.n

    qg0, ss1, ts1, vs1 = QG.n, SS.n, TS.n, VS.n
    ts_sw = ts1 + 1
    vs_sw = vs1 + 1
    xTr = d_xT.rearrange("(c p) n -> p c n", p=128)

    ss_sw = ss1 + 1

    def a2_sq(hc):
        return ss_sw + 2 * hc + 1

    def a2_ev(hc):
        return ss_sw + 2 * hc + 2

    @block.gpsimd
    def _(g):
        for hc in range(16):
            if hc >= 2:
                g.wait_ge(SS.h, a2_sq(hc - 2))
                g.wait_ge(TS.h, ts_sw + (hc - 2) + 1)
            r = hc % 2
            QG.inc(g.dma_start(
                out=xtc[:, r * 8192:(r + 1) * 8192].rearrange("p (c n) -> p c n", c=8),
                in_=xTr[:, :, hc * 1024:(hc + 1) * 1024]), 16)
    assert QG.n == qg0 + 16 * 16

    @block.scalar
    def _(s):
        s.wait_ge(TS.h, ts_sw)
        SS.inc(s.copy(r_sw[:], psB[:, 0:8]))
        assert SS.n == ss_sw
        for hc in range(16):
            r = hc % 2
            pbank = psA if hc % 2 == 0 else psB
            s.wait_ge(QG.h, qg0 + 16 * (hc + 1))
            if hc >= 1:
                s.wait_ge(TS.h, ts_sw + hc)
            SS.inc(s.activation(xsq[:], xtc[:, r * 8192:(r + 1) * 8192], AF.Square))
            s.wait_ge(TS.h, ts_sw + hc + 1)
            SS.inc(s.copy(zraw[:, hc * 80:(hc + 1) * 80], pbank[:, 0:80]))
    assert SS.n == ss_sw + 32

    @block.tensor
    def _(t):
        t.wait_ge(QG.h, m_gconsts)
        for dc in range(8):
            mm = t.matmul(psB[:, 0:8], c_ones[:], w_pw9[:, dc * 9 + 1:dc * 9 + 9],
                          start=(dc == 0), stop=(dc == 7))
        TS.inc(mm)
        for hc in range(16):
            r = hc % 2
            pbank = psA if hc % 2 == 0 else psB
            t.wait_ge(QG.h, qg0 + 16 * (hc + 1))
            t.wait_ge(SS.h, a2_sq(hc))
            if hc >= 2:
                t.wait_ge(SS.h, a2_ev(hc - 2))
            if hc == 1:
                t.wait_ge(SS.h, ss_sw)
            for i in range(8):
                for dc in range(8):
                    t.matmul(pbank[:, i * 9:i * 9 + 9],
                             xtc[:, r * 8192 + dc * 1024 + i * 128:
                                 r * 8192 + dc * 1024 + (i + 1) * 128],
                             w_pw9[:, dc * 9:(dc + 1) * 9],
                             start=(dc == 0), stop=(dc == 7))
                for dc in range(8):
                    mm = t.matmul(pbank[:, 72 + i:73 + i],
                                  xsq[:, dc * 1024 + i * 128:dc * 1024 + (i + 1) * 128],
                                  c_ones[:, 0:1],
                                  start=(dc == 0), stop=(dc == 7))
            TS.inc(mm)
    assert TS.n == ts_sw + 16
    ss_a2end = SS.n

    zr3 = zraw[:].rearrange("p (h s) -> p h s", h=16)
    zr4 = zr3[:, :, 0:72].rearrange("p h (i f) -> p h i f", f=9)
    muAP = zr4[:, :, :, 0]
    ssAP = zr3[:, :, 72:80]
    t1v = lambda t: t[:].rearrange("p (h i) -> p h i", h=16)

    @block.vector
    def _(v):
        v.wait_ge(SS.h, ss_a2end)
        v.drain()
        v.tensor_scalar(t1v(r_t1), muAP, 1.0 / D, None, ALU.mult)
        v.drain()
        v.tensor_scalar(t1v(r_t2), ssAP, 1.0 / D, None, ALU.mult)
        v.tensor_tensor(r_t3[:], r_t1[:], r_t1[:], ALU.mult)
        v.tensor_tensor(r_t2[:], r_t2[:], r_t3[:], ALU.subtract)
        v.tensor_scalar(r_t2[:], r_t2[:], 1e-5, None, ALU.add)
        VS.inc(v.reciprocal(r_t2[:], r_t2[:]))
    vs_var = VS.n

    @block.scalar
    def _(s):
        s.wait_ge(VS.h, vs_var)
        SS.inc(s.activation(r_t2[:], r_t2[:], AF.Sqrt))
    ss_rstd = SS.n

    @block.vector
    def _(v):
        v.wait_ge(SS.h, ss_rstd)
        v.wait_ge(QS.h, m_sconsts)
        for e in range(E):
            pe = zr4[:, :, :, 1 + e]
            zsl = r_z[:, e * NR:(e + 1) * NR]
            v.tensor_scalar(r_t3[:], r_t1[:], r_sw[:, e:e + 1], None, ALU.mult)
            v.drain()
            v.tensor_tensor(zsl.rearrange("p (h i) -> p h i", h=16), pe,
                            t1v(r_t3), ALU.subtract)
            v.drain()
            v.tensor_tensor(zsl, zsl, r_t2[:], ALU.mult)
            last = v.tensor_scalar(zsl, zsl, w_pbP[:, e:e + 1], None, ALU.add)
        VS.inc(last)
    vs_z = VS.n

    @block.scalar
    def _(s):
        s.wait_ge(VS.h, vs_z)
        SS.inc(s.activation(r_z[:], r_z[:], AF.Sigmoid))
    ss_sig = SS.n

    @block.vector
    def _(v):
        v.wait_ge(SS.h, ss_sig)
        v.wait_ge(QS.h, m_sconsts)
        for e in range(E):
            zsl = r_z[:, e * NR:(e + 1) * NR]
            v.tensor_scalar(zsl, zsl, w_ebP[:, e:e + 1], None, ALU.add)
        VS.inc(v.tensor_scalar(r_z[:], r_z[:], 1e-12, None, ALU.max))
    vs_clip = VS.n

    @block.scalar
    def _(s):
        s.wait_ge(VS.h, vs_clip)
        SS.inc(s.activation(r_z[:], r_z[:], AF.Ln))
    ss_ln = SS.n

    @block.vector
    def _(v):
        v.wait_ge(SS.h, ss_ln)
        v.tensor_tensor(r_a[:, 0:512], r_z[:, 0:512], r_z[:, 512:1024], ALU.max)
        v.tensor_tensor(r_a[:, 0:256], r_a[:, 0:256], r_a[:, 256:512], ALU.max)
        v.tensor_tensor(r_m1[:], r_a[:, 0:128], r_a[:, 128:256], ALU.max)
        for e in range(E):
            v.tensor_copy(r_b[:, e * NR:(e + 1) * NR], r_m1[:])
        v.drain()
        v.tensor_tensor(r_u8[:], r_z[:], r_b[:], ALU.is_equal)
        v.tensor_scalar(r_a[:], r_io[:], 0.0, 99.0, ALU.mult, ALU.add)
        v.select(r_a[:], r_u8[:], r_io[:], r_a[:])
        v.tensor_tensor(r_a[:, 0:512], r_a[:, 0:512], r_a[:, 512:1024], ALU.min)
        v.tensor_tensor(r_a[:, 0:256], r_a[:, 0:256], r_a[:, 256:512], ALU.min)
        v.tensor_tensor(r_e1[:], r_a[:, 0:128], r_a[:, 128:256], ALU.min)
        for e in range(E):
            v.tensor_copy(r_b[:, e * NR:(e + 1) * NR], r_e1[:])
        v.drain()
        v.tensor_tensor(r_a[:], r_io[:], r_b[:], ALU.is_equal)
        v.scalar_tensor_tensor(r_b[:], r_a[:], NEG, r_z[:], ALU.mult, ALU.add)
        v.tensor_tensor(r_a[:, 0:512], r_b[:, 0:512], r_b[:, 512:1024], ALU.max)
        v.tensor_tensor(r_a[:, 0:256], r_a[:, 0:256], r_a[:, 256:512], ALU.max)
        v.tensor_tensor(r_m2[:], r_a[:, 0:128], r_a[:, 128:256], ALU.max)
        for e in range(E):
            v.tensor_copy(r_a[:, e * NR:(e + 1) * NR], r_m2[:])
        v.drain()
        v.tensor_tensor(r_u8[:], r_b[:], r_a[:], ALU.is_equal)
        v.tensor_scalar(r_a[:], r_io[:], 0.0, 99.0, ALU.mult, ALU.add)
        v.select(r_a[:], r_u8[:], r_io[:], r_a[:])
        v.tensor_tensor(r_a[:, 0:512], r_a[:, 0:512], r_a[:, 512:1024], ALU.min)
        v.tensor_tensor(r_a[:, 0:256], r_a[:, 0:256], r_a[:, 256:512], ALU.min)
        v.tensor_tensor(r_e2[:], r_a[:, 0:128], r_a[:, 128:256], ALU.min)
        VS.inc(v.tensor_tensor(r_t1[:], r_m1[:], r_m2[:], ALU.subtract))
    vs_top = VS.n

    @block.scalar
    def _(s):
        s.wait_ge(VS.h, vs_top)
        s.activation(r_w1[:], r_t1[:], AF.Sigmoid)
        SS.inc(s.activation(r_w2[:], r_t1[:], AF.Sigmoid, scale=-1.0))
    ss_w = SS.n

    @block.vector
    def _(v):
        v.wait_ge(SS.h, ss_w)
        for e in range(E):
            fe = float(e)
            wsl = r_a[:, e * NR:(e + 1) * NR]
            v.tensor_scalar(r_t1[:], r_e1[:], fe, None, ALU.is_equal)
            v.tensor_scalar(r_t2[:], r_e2[:], fe, None, ALU.is_equal)
            v.tensor_tensor(r_t3[:], r_t1[:], r_w1[:], ALU.mult)
            v.tensor_tensor(wsl, r_t2[:], r_w2[:], ALU.mult)
            v.tensor_tensor(wsl, wsl, r_t3[:], ALU.add)
            v.tensor_tensor(r_t3[:], r_t1[:], r_t2[:], ALU.add)
            v.tensor_scalar(r_t3[:], r_t3[:], 1.0, None, ALU.subtract)
            last = v.tensor_tensor(wsl, wsl, r_t3[:], ALU.add)
        VS.inc(last)
    vs_wb = VS.n

    NBIS = 40
    DELTA = [2.0 ** (-k) for k in range(NBIS)]
    s_mid, v_cmp, t_cnt_m, v_bit, t_acc = {}, {}, {}, {}, {}
    scur, vcur, tcur = SS.n, VS.n, TS.n
    for k in range(NBIS):
        scur += 1; s_mid[k] = scur
        vcur += 1; v_cmp[k] = vcur
        tcur += 1; t_cnt_m[k] = tcur
        if k < NBIS - 0:
            vcur += 1; v_bit[k] = vcur
            tcur += 1; t_acc[k] = tcur
    scur += 1
    s_fin = scur
    vcur += 1; v_cmp[NBIS] = vcur
    tcur += 1; t_cnt_m[NBIS] = tcur
    scur += 1
    s_rr = scur

    @block.scalar
    def _(s):
        for k in range(NBIS):
            if k == 0:
                s.wait_ge(VS.h, vs_wb)
                s.wait_ge(QS.h, m_sconsts)
                SS.inc(s.activation(r_midt[:], w_ebP[:], AF.Identity,
                                    bias=w_bisc[:, 0:1], scale=0.0))
            else:
                s.wait_ge(TS.h, t_acc[k - 1])
                SS.inc(s.activation(r_midt[:], psA[:, 0:8], AF.Identity,
                                    bias=w_bisc[:, k:k + 1]))
        s.wait_ge(TS.h, t_acc[NBIS - 1])
        s.copy(r_loS[:], psA[:, 0:8])
        SS.inc(s.activation(r_hiS[:], psA[:, 0:8], AF.Identity,
                            bias=w_bisc[:, NBIS - 1:NBIS]))
        assert SS.n == s_fin
        s.wait_ge(TS.h, t_cnt_m[NBIS])
        SS.inc(s.activation(r_rrS[:], psB[:, 0:8], AF.Identity,
                            scale=-1.0, bias=w_bisc[:, NBIS:NBIS + 1]))
        assert SS.n == s_rr

    @block.vector
    def _(v):
        for k in range(NBIS + 1):
            v.wait_ge(SS.h, s_mid[k] if k < NBIS else s_fin)
            th = r_midt if k < NBIS else r_hiS
            for e in range(E):
                v.tensor_scalar(r_t1[:], r_a[:, e * NR:(e + 1) * NR],
                                th[:, e:e + 1], 0.0, ALU.is_ge, ALU.add,
                                accum_out=r_cnt[:, e:e + 1])
            VS.inc(v.drain())
            assert VS.n == v_cmp[k]
            if k < NBIS:
                v.wait_ge(TS.h, t_cnt_m[k])
                VS.inc(v.tensor_scalar(r_bit[:], psB[:, 0:8], float(CAP), DELTA[k],
                                       ALU.is_ge, ALU.mult))
                assert VS.n == v_bit[k]

    @block.tensor
    def _(t):
        for k in range(NBIS + 1):
            t.wait_ge(VS.h, v_cmp[k])
            TS.inc(t.matmul(psB[:, 0:8], c_ones[:], r_cnt[:], start=True, stop=True))
            assert TS.n == t_cnt_m[k]
            if k < NBIS:
                t.wait_ge(VS.h, v_bit[k])
                TS.inc(t.matmul(psA[:, 0:8], c_ident[:], r_bit[:],
                                start=(k == 0), stop=(k == NBIS - 1)))
                assert TS.n == t_acc[k]

    @block.vector
    def _(v):
        v.wait_ge(SS.h, s_fin)
        for e in range(E):
            v.tensor_scalar(r_t1[:], r_a[:, e * NR:(e + 1) * NR],
                            r_loS[:, e:e + 1], None, ALU.is_ge)
            v.tensor_scalar(r_t2[:], r_a[:, e * NR:(e + 1) * NR],
                            r_hiS[:, e:e + 1], None, ALU.is_lt)
            last = v.tensor_tensor(r_eqb[:, e * NR:(e + 1) * NR], r_t1[:], r_t2[:],
                                   ALU.mult)
        VS.inc(last)
    vs_eq = VS.n

    @block.tensor
    def _(t):
        t.wait_ge(VS.h, vs_eq)
        t.matmul(psC[:, 0:512], c_ones[:], r_eqb[:, 0:512], start=True, stop=True)
        t.matmul(psC[:, 512:1024], c_ones[:], r_eqb[:, 512:1024], start=True, stop=True)
        t.matmul(psD[:, 0:512], c_lt[:], r_eqb[:, 0:512], start=True, stop=True)
        TS.inc(t.matmul(psD[:, 512:1024], c_lt[:], r_eqb[:, 512:1024],
                        start=True, stop=True))
    ts_eq = TS.n

    @block.vector
    def _(v):
        v.wait_ge(TS.h, ts_eq)
        v.wait_ge(SS.h, s_rr)
        v.tensor_copy(r_b[:], psC[:])
        for e in range(E):
            sl = r_z[:, e * NR:(e + 1) * NR]
            v.memset(sl[:, 0:1], 0.0)
            v.tensor_copy(sl[:, 1:NR], r_b[:, e * NR:e * NR + NR - 1])
        v.drain()
        src_, dst_ = r_z, r_b
        for st in range(7):
            sh = 1 << st
            for e in range(E):
                ssl = src_[:, e * NR:(e + 1) * NR]
                dsl = dst_[:, e * NR:(e + 1) * NR]
                v.tensor_copy(dsl[:, 0:sh], ssl[:, 0:sh])
                v.tensor_tensor(dsl[:, sh:NR], ssl[:, sh:NR], ssl[:, 0:NR - sh], ALU.add)
            src_, dst_ = dst_, src_
            v.drain()
        rank = src_
        v.tensor_tensor(rank[:], rank[:], psD[:], ALU.add)
        for slot, (we, ee, gg) in enumerate(((r_w1, r_e1, r_g1), (r_w2, r_e2, r_g2))):
            v.memset(r_ve[:], 0.0)
            v.memset(r_re[:], 0.0)
            v.memset(r_rk[:], 0.0)
            v.memset(r_t3[:], 0.0)
            v.drain()
            for e in range(E):
                fe = float(e)
                v.tensor_scalar(r_t1[:], ee[:], fe, None, ALU.is_equal)
                v.scalar_tensor_tensor(r_t2[:], r_t1[:], r_loS[:, e:e + 1], r_ve[:],
                                       ALU.mult, ALU.add)
                v.tensor_copy(r_ve[:], r_t2[:])
                v.scalar_tensor_tensor(r_t2[:], r_t1[:], r_hiS[:, e:e + 1], r_t3[:],
                                       ALU.mult, ALU.add)
                v.tensor_copy(r_t3[:], r_t2[:])
                v.scalar_tensor_tensor(r_t2[:], r_t1[:], r_rrS[:, e:e + 1], r_re[:],
                                       ALU.mult, ALU.add)
                v.tensor_copy(r_re[:], r_t2[:])
                v.tensor_tensor(r_t2[:], r_t1[:], rank[:, e * NR:(e + 1) * NR], ALU.mult)
                v.tensor_tensor(r_rk[:], r_rk[:], r_t2[:], ALU.add)
            v.drain()
            v.tensor_tensor(r_t1[:], we[:], r_t3[:], ALU.is_ge)
            v.tensor_tensor(r_t2[:], we[:], r_ve[:], ALU.is_ge)
            v.tensor_tensor(r_t2[:], r_t2[:], r_t1[:], ALU.subtract)
            v.tensor_tensor(r_t3[:], r_rk[:], r_re[:], ALU.is_lt)
            v.tensor_tensor(r_t2[:], r_t2[:], r_t3[:], ALU.mult)
            v.tensor_tensor(r_t1[:], r_t1[:], r_t2[:], ALU.add)
            v.tensor_tensor(gg[:], we[:], r_t1[:], ALU.mult)
            v.drain()
        for e in range(E):
            fe = float(e)
            gsl = r_ga[:, e * NR:(e + 1) * NR]
            v.tensor_scalar(r_t1[:], r_e1[:], fe, None, ALU.is_equal)
            v.tensor_tensor(gsl, r_t1[:], r_g1[:], ALU.mult)
            v.tensor_scalar(r_t1[:], r_e2[:], fe, None, ALU.is_equal)
            v.tensor_tensor(r_t2[:], r_t1[:], r_g2[:], ALU.mult)
            v.tensor_tensor(gsl, gsl, r_t2[:], ALU.add)
        v.drain()
        v.tensor_scalar(r_t1[:], c_io16[:], w_cid[:, 0:1], None, ALU.is_equal)
        for e in range(E):
            v.tensor_tensor(r_t2[:], r_ga[:, e * NR:(e + 1) * NR], r_t1[:], ALU.mult)
            v.drain()
            src3 = r_t2[:].rearrange("p (c t) -> p c t", c=8).transpose([0, 2, 1])
            dst3 = gates[:, e * NT:(e + 1) * NT].rearrange("p (t o) -> p t o", o=1)
            last = v.tensor_reduce(dst3, src3, AX.X, ALU.add)
            v.drain()
        VS.inc(last)
    vs_gates = VS.n

    if d_dbg:
        dbg_map = {"z": r_z, "w1": r_w1, "w2": r_w2, "e1": r_e1, "e2": r_e2,
                   "V": r_loS, "r": r_rrS, "g1": r_g1, "g2": r_g2, "gates": gates,
                   "W": r_a, "m1": r_m1, "m2": r_m2, "xnT": xnT}

        @block.sync
        def _(sync):
            sync.wait_ge(VS.h, vs_gates)
            for name in d_dbg:
                if name in dbg_map and dbg_map[name].ap().dtype == F32:
                    QS.inc(sync.dma_start(out=d_dbg[name][:], in_=dbg_map[name][:]), 16)

        @block.gpsimd
        def _(g):
            g.wait_ge(VS.h, vs_gates)
            for name in d_dbg:
                if name in dbg_map and dbg_map[name].ap().dtype != F32:
                    QG.inc(g.dma_start(out=d_dbg[name][:], in_=dbg_map[name][:]), 16)

    ascope.close()
    if stop_after == "A":
        stack.close()
        return nc, d_dbg

    accscope = ExitStack()
    acc = accscope.enter_context(nc.sbuf_tensor("acc", [128, NT * D], F32))
    bscope = ExitStack()
    bsb = lambda n, s, dt_, **kw: bscope.enter_context(nc.sbuf_tensor(n, s, dt_, **kw))
    hbuf = bsb("hbuf", [128, 16 * T], BF16)
    wf1 = bsb("wf1", [128, 2 * 1024], BF16)
    wf2 = bsb("wf2", [128, 16 * D], BF16)
    w_sh2bB = bsb("w_sh2bB", [128, D], F32)

    units = [("sq", q) for q in range(4)] + [("ex", e) for e in range(E)]
    if b_units is not None:
        units = units[:b_units]

    def udims(u):
        kind, idx = units[u]
        ehu = 1024 if kind == "sq" else EHE
        return kind, idx, ehu, ehu // 128

    ts_b0, ss_b0, vs_b0, qg_b0 = TS.n, SS.n, VS.n, QG.n
    unit_ts = []
    tc_ = ts_b0
    for u in range(len(units)):
        _, _, _, nsl = udims(u)
        unit_ts.append(tc_)
        tc_ += nsl * 4 + 16
    ts_b_end = tc_
    gmap = []
    for u in range(len(units)):
        _, _, _, nsl = udims(u)
        for gi in range(nsl * 4):
            gmap.append((u, gi))
    NGRP = len(gmap)
    ss_of_group = {}
    cnt = ss_b0
    for g_ in range(NGRP):
        cnt += 1
        ss_of_group[g_] = cnt
    ss_b_end = cnt
    ts_of_group = {}
    for g_, (u, gi) in enumerate(gmap):
        ts_of_group[g_] = unit_ts[u] + gi + 1
    vs_of_gt = {gt: vs_b0 + gt + 1 for gt in range(16 * len(units))}
    vs_b_end = vs_b0 + 16 * len(units)
    smap = []
    qg_of_unit_fc2 = {}
    qg_of_slice = {}
    qq = qg_b0
    for u in range(len(units)):
        _, _, _, nsl = udims(u)
        qq += 16
        qg_of_slice[len(smap)] = qq
        smap.append((u, 0))
        qq += 16
        qg_of_slice[len(smap)] = qq
        smap.append((u, 1))
        qq += 16
        qg_of_unit_fc2[u] = qq
        for sl in range(2, nsl):
            qq += 16
            qg_of_slice[len(smap)] = qq
            smap.append((u, sl))
    qg_b_end = qq
    ts_slice_done = {i: unit_ts[u] + (sl + 1) * 4 for i, (u, sl) in enumerate(smap)}

    @block.sync
    def _(sync):
        QS.inc(sync.dma_start(out=w_sh2bB[:], in_=d_sh2bB[:]), 16)
    qs_sh2b = QS.n

    @block.gpsimd
    def _(g):
        sno = 0

        def load_slice(kind, idx, sl):
            nonlocal sno
            if sno >= 2:
                g.wait_ge(TS.h, ts_slice_done[sno - 2])
            rr = sno % 2
            if kind == "ex":
                src1 = d_fc1T[idx][:, sl * 128:(sl + 1) * 128].rearrange(
                    "(c p) h -> p c h", p=128)
            else:
                src1 = d_sh1T[:, idx * 1024 + sl * 128:idx * 1024 + (sl + 1) * 128
                              ].rearrange("(c p) h -> p c h", p=128)
            QG.inc(g.dma_start(
                out=wf1[:, rr * 1024:(rr + 1) * 1024].rearrange(
                    "p (c h) -> p c h", c=8),
                in_=src1), 16)
            sno += 1

        for u in range(len(units)):
            kind, idx, ehu, nsl = udims(u)
            load_slice(kind, idx, 0)
            load_slice(kind, idx, 1)
            if u >= 1:
                g.wait_ge(TS.h, unit_ts[u])
            if kind == "ex":
                src2 = d_fc2T[idx].rearrange("(k p) d -> p k d", p=128)
                QG.inc(g.dma_start(
                    out=wf2[:, 0:16 * D].rearrange("p (k d) -> p k d", k=16),
                    in_=src2), 16)
            else:
                src2 = d_sh2T[idx * 1024:(idx + 1) * 1024].rearrange(
                    "(k p) d -> p k d", p=128)
                QG.inc(g.dma_start(
                    out=wf2[:, 0:8 * D].rearrange("p (k d) -> p k d", k=8),
                    in_=src2), 16)
            for sl in range(2, nsl):
                load_slice(kind, idx, sl)
    assert QG.n == qg_b_end

    @block.tensor
    def _(t):
        t.wait_ge(VS.h, vs_gates)
        gglob = 0
        sno = 0
        for u in range(len(units)):
            kind, idx, ehu, nsl = udims(u)
            nkc = ehu // 128
            for sl in range(nsl):
                t.wait_ge(QG.h, qg_of_slice[sno])
                for tcx in range(4):
                    pbank = psA if (gglob % 2 == 0) else psB
                    if gglob >= 2:
                        t.wait_ge(SS.h, ss_of_group[gglob - 2])
                    rr = sno % 2
                    for dc in range(8):
                        mm = t.matmul(
                            pbank[:, 0:512],
                            wf1[:, rr * 1024 + dc * 128:rr * 1024 + (dc + 1) * 128],
                            xnT[:, dc * T + tcx * 512:dc * T + (tcx + 1) * 512],
                            start=(dc == 0), stop=(dc == 7))
                    TS.inc(mm)
                    gglob += 1
                sno += 1
            t.wait_ge(SS.h, ss_of_group[gglob - 1])
            t.wait_ge(QG.h, qg_of_unit_fc2[u])
            for tile in range(16):
                gt = 16 * u + tile
                pbank = psC if (gt % 2 == 0) else psD
                if fc2_serial:
                    if gt >= 1:
                        t.wait_ge(VS.h, vs_of_gt[gt - 1])
                elif gt >= 2:
                    t.wait_ge(VS.h, vs_of_gt[gt - 2])
                kcs = list(range(nkc))[::-1] if kc_rev else list(range(nkc))
                if fc2_nh_outer:
                    for nh in range(2):
                        for ki, kc in enumerate(kcs):
                            laststop = (ki == nkc - 1) and (kind != "ex")
                            mm = t.matmul(
                                pbank[:, nh * 512:(nh + 1) * 512],
                                hbuf[:, kc * T + tile * 128:kc * T + (tile + 1) * 128],
                                wf2[:, kc * D + nh * 512:kc * D + (nh + 1) * 512],
                                start=(ki == 0), stop=laststop)
                        if kind == "ex":
                            mm = t.matmul(pbank[:, nh * 512:(nh + 1) * 512],
                                          c_e8[:, idx * 128:(idx + 1) * 128],
                                          w_fc2bA[:, nh * 512:(nh + 1) * 512],
                                          start=False, stop=True)
                else:
                    for ki, kc in enumerate(kcs):
                        laststop = (ki == nkc - 1) and (kind != "ex")
                        kslot = ((kc + 1) % nkc) if slot_shift else kc
                        for nh in range(2):
                            if fc2_dup0 and ki == 0:
                                t.matmul(
                                    pbank[:, nh * 512:(nh + 1) * 512],
                                    hbuf[:, kslot * T + tile * 128:kslot * T + (tile + 1) * 128],
                                    wf2[:, kc * D + nh * 512:kc * D + (nh + 1) * 512],
                                    start=True, stop=False)
                            mm = t.matmul(
                                pbank[:, nh * 512:(nh + 1) * 512],
                                hbuf[:, kslot * T + tile * 128:kslot * T + (tile + 1) * 128],
                                wf2[:, kc * D + nh * 512:kc * D + (nh + 1) * 512],
                                start=(ki == 0), stop=laststop)
                    if kind == "ex":
                        for nh in range(2):
                            mm = t.matmul(pbank[:, nh * 512:(nh + 1) * 512],
                                          c_e8[:, idx * 128:(idx + 1) * 128],
                                          w_fc2bA[:, nh * 512:(nh + 1) * 512],
                                          start=False, stop=True)
                TS.inc(mm)
        assert TS.n == ts_b_end

    @block.scalar
    def _(s):
        for g_, (u, gi) in enumerate(gmap):
            kind, idx, ehu, nsl = udims(u)
            pbank = psA if (g_ % 2 == 0) else psB
            kglob = gi // 4
            tcx = gi % 4
            if gi == 0 and u >= 1:
                s.wait_ge(TS.h, unit_ts[u])
            s.wait_ge(TS.h, ts_of_group[g_])
            bias = (w_fc1bP[:, idx * 16 + kglob:idx * 16 + kglob + 1]
                    if kind == "ex" else
                    w_sh1bP[:, idx * 8 + kglob:idx * 8 + kglob + 1])
            kslot = ((kglob + 1) % nsl) if slot_shift else kglob
            SS.inc(s.activation(
                hbuf[:, kslot * T + tcx * 512:kslot * T + (tcx + 1) * 512],
                pbank[:, 0:512], AF.Gelu, bias=bias))
        assert SS.n == ss_b_end

    @block.vector
    def _(v):
        for u in range(len(units)):
            kind, idx, ehu, nsl = udims(u)
            for tile in range(16):
                gt = 16 * u + tile
                pbank = psC if (gt % 2 == 0) else psD
                v.wait_ge(TS.h, unit_ts[u] + nsl * 4 + tile + 1)
                asl = acc[:, tile * D:(tile + 1) * D]
                if u == 0:
                    if tile == 0:
                        v.wait_ge(QS.h, qs_sh2b)
                    VS.inc(v.tensor_tensor(asl, pbank[:], w_sh2bB[:], ALU.add))
                elif kind == "sq":
                    VS.inc(v.tensor_tensor(asl, pbank[:], asl, ALU.add))
                else:
                    VS.inc(v.scalar_tensor_tensor(
                        asl, pbank[:], gates[:, idx * NT + tile:idx * NT + tile + 1],
                        asl, ALU.mult, ALU.add))
        assert VS.n == vs_b_end


    if "hbuf" in d_dbg or "wf2" in d_dbg:
        @block.gpsimd
        def _(g):
            g.wait_ge(VS.h, vs_b_end)
            if "hbuf" in d_dbg:
                QG.inc(g.dma_start(out=d_dbg["hbuf"][:], in_=hbuf[:]), 16)
            if "wf2" in d_dbg:
                QG.inc(g.dma_start(out=d_dbg["wf2"][:], in_=wf2[:]), 16)
    if stop_after == "B":
        bscope.close()
        accscope.close()
        stack.close()
        return nc, d_dbg

    TCH = 256
    NTC = T // TCH
    NB = TCH // 32
    NM = TCH // 128

    spbuf = wf1
    ss_c0, qs_c0 = SS.n, QS.n
    gspr = d_gsp.rearrange("(n p) d -> n p d", p=128)
    yr = d_y.rearrange("(n p) d -> n p d", p=128)

    @block.scalar
    def _(s):
        for i in range(NT):
            s.wait_ge(VS.h, vs_of_gt[16 * (len(units) - 1) + i])
            if i >= 2:
                s.wait_ge(QS.h, qs_c0 + 16 * (i - 1))
            SS.inc(s.activation(spbuf[:, (i % 2) * D:(i % 2 + 1) * D],
                                acc[:, i * D:(i + 1) * D], AF.Silu))
    assert SS.n == ss_c0 + NT

    @block.sync
    def _(sync):
        for i in range(NT):
            sync.wait_ge(SS.h, ss_c0 + i + 1)
            QS.inc(sync.dma_start(out=gspr[i],
                                  in_=spbuf[:, (i % 2) * D:(i % 2 + 1) * D]), 16)
        if "acc2" in d_dbg:
            sync.wait_ge(VS.h, vs_b_end)
            QS.inc(sync.dma_start(out=d_dbg["acc2"][:], in_=acc[:]), 16)
    qs_spill_end = QS.n
    assert qs_spill_end == qs_c0 + 16 * NT + (16 if "acc2" in d_dbg else 0)

    bscope.close()
    accscope.close()
    if cstop == "spill":
        stack.close()
        return nc, d_dbg

    cscope = ExitStack()
    csb = lambda n, s, dt_, **kw: cscope.enter_context(nc.sbuf_tensor(n, s, dt_, **kw))
    wqkv = csb("wqkv", [128, 2 * 16384], BF16)
    wpT = csb("wpT", [64, 16 * 1024], BF16)
    qBD = csb("qBD", [128, 16 * TCH], BF16)
    kBD = csb("kBD", [128, 16 * TCH], BF16)
    vBDT = csb("vBDT", [128, 16 * TCH], BF16)
    svT = csb("svT", [64, 16 * TCH], BF16)
    expb = csb("expb", [128, 512], F32)
    nmax = csb("nmax", [128, 4], F32)
    sums = csb("sums", [128, 4], F32)
    sums2 = csb("sums2", [128, 4], F32)
    expc = csb("expc", [128, 512], F32)
    c_bdneg = csb("c_bdneg", [128, 128], F32)
    attnb = csb("attnb", [128, 512], BF16)
    attnT = csb("attnT", [128, 512], BF16)
    vBDs = csb("vBDs", [128, 256], BF16)
    gbuf = csb("gbuf", [128, D], BF16)
    xbuf = csb("xbuf", [128, D], F32)
    obuf = csb("obuf", [128, D], F32)
    w_bprojB = csb("w_bprojB", [128, D], F32)
    c_bdmask = csb("c_bdmask", [128, 128], F32)

    qg_c0 = QG.n
    ts_c0, vs_c0, ss_c1 = TS.n, VS.n, SS.n
    TE_PER_TC = 8 + 16 + 16 + NB * 7 + NM
    V_PER_TC = 8 + 16 + 16 + NB * 4 + NM
    S_PER_TC = 2 * NB

    def t_q(tc, hp): return ts_c0 + TE_PER_TC * tc + hp + 1
    def t_k(tc, g): return ts_c0 + TE_PER_TC * tc + 8 + g + 1
    def t_v(tc, g): return ts_c0 + TE_PER_TC * tc + 24 + g + 1
    def t_sc(tc, gb, j): return ts_c0 + TE_PER_TC * tc + 40 + 7 * gb + j + 1
    def t_at(tc, gb): return ts_c0 + TE_PER_TC * tc + 40 + 7 * gb + 5
    def t_vt(tc, gb): return ts_c0 + TE_PER_TC * tc + 40 + 7 * gb + 6
    def t_av(tc, gb): return ts_c0 + TE_PER_TC * tc + 40 + 7 * gb + 7
    def t_op(tc, m): return ts_c0 + TE_PER_TC * tc + 40 + 7 * NB + m + 1

    def v_q(tc, hp): return vs_c0 + V_PER_TC * tc + hp + 1
    def v_k(tc, g): return vs_c0 + V_PER_TC * tc + 8 + g + 1
    def v_v(tc, g): return vs_c0 + V_PER_TC * tc + 24 + g + 1
    def v_nm(tc, gb): return vs_c0 + V_PER_TC * tc + 40 + 4 * gb + 1
    def v_rc(tc, gb): return vs_c0 + V_PER_TC * tc + 40 + 4 * gb + 2
    def v_ae(tc, gb): return vs_c0 + V_PER_TC * tc + 40 + 4 * gb + 3
    def v_sv(tc, gb): return vs_c0 + V_PER_TC * tc + 40 + 4 * gb + 4
    def v_fin(tc, m): return vs_c0 + V_PER_TC * tc + 40 + 4 * NB + m + 1

    def s_exp(tc, gb): return ss_c1 + S_PER_TC * tc + 2 * gb + 1
    def s_no(tc, gb): return ss_c1 + S_PER_TC * tc + 2 * gb + 2

    def qg_w(m): return qg_c0 + 16 + 16 * (m + 1)

    if cstop and cstop.startswith("tc"):
        NTC_E = int(cstop[2:]) + 1
    elif cstop:
        NTC_E = 1
    else:
        NTC_E = NTC

    @block.gpsimd
    def _(g):
        g.wait_ge(SS.h, ss_c0 + NT)
        if "acc2" in d_dbg:
            g.wait_ge(QS.h, qs_spill_end)
        QG.inc(g.dma_start(out=wpT[:], in_=d_wpT[:]), 16)
        for tc in range(NTC_E):
            for kind in range(3):
                m = 3 * tc + kind
                if m >= 2:
                    tc2, kd2 = divmod(m - 2, 3)
                    done = (t_q(tc2, 7) if kd2 == 0 else
                            t_k(tc2, 15) if kd2 == 1 else t_v(tc2, 15))
                    g.wait_ge(TS.h, done)
                sl = (m % 2) * 16384
                if kind == 0:
                    QG.inc(g.dma_start(
                        out=wqkv[:, sl:sl + 8 * 1024].rearrange("p (c n) -> p c n", c=8),
                        in_=d_wqT.rearrange("(c p) n -> p c n", p=128)), 16)
                else:
                    srcw = d_wkTd if kind == 1 else d_wvTd
                    QG.inc(g.dma_start(
                        out=wqkv[:, sl:sl + 16384].rearrange("p (c n) -> p c n", c=8),
                        in_=srcw.rearrange("(c p) n -> p c n", p=128)), 16)
    qg_c_end = QG.n

    qs_load = {}
    qs_store = {}

    @block.sync
    def _(sync):
        QS.inc(sync.dma_start(out=w_bprojB[:], in_=d_bprojB[:]), 16)
        QS.inc(sync.dma_start(out=c_bdmask[:], in_=d_bdmask[:]), 16)
        QS.inc(sync.dma_start(out=c_bdneg[:], in_=d_bdneg[:]), 16)
        n_gmt = (16 if not cstop else
                 (2 * NTC_E if cstop.startswith("tc") else 0))
        for gmt in range(n_gmt):
            if gmt >= 1:
                tcp, mp = divmod(gmt - 1, NM)
                sync.wait_ge(VS.h, v_fin(tcp, mp))
                QS.inc(sync.dma_start(out=yr[gmt - 1], in_=obuf[:]), 16)
                qs_store[gmt - 1] = QS.n
            QS.inc(sync.dma_start(out=gbuf[:], in_=gspr[gmt]), 16)
            QS.inc(sync.dma_start(out=xbuf[:], in_=xLr[gmt]), 16)
            qs_load[gmt] = QS.n
        if not cstop:
            tcp, mp = divmod(15, NM)
            sync.wait_ge(VS.h, v_fin(tcp, mp))
            QS.inc(sync.dma_start(out=yr[15], in_=obuf[:]), 16)
            qs_store[15] = QS.n
        sync.wait_ge(QS.h, QS.n)

    _ord = {"qkv": 1, "scores": 2, "softmax": 3, "attnv": 4, "outproj": 5,
            None: 9}
    LVL = 9 if (cstop and cstop.startswith("tc")) else _ord.get(cstop, 9)
    NB_E = 1 if cstop in ("scores", "softmax", "attnv") else NB

    @block.tensor
    def _(t):
        t.wait_ge(SS.h, ss_b_end)
        for tc in range(NTC_E):
            for hp in range(8):
                gqi = hp
                pbank = psA if gqi % 2 == 0 else psB
                t.wait_ge(QG.h, qg_w(3 * tc + 0))
                if gqi >= 2:
                    t.wait_ge(VS.h, v_q(tc, hp - 2))
                elif tc >= 1:
                    t.wait_ge(VS.h, v_fin(tc - 1, NM - 1))
                sl = ((3 * tc + 0) % 2) * 16384
                for dc in range(8):
                    mm = t.matmul(pbank[:, 0:TCH],
                                  wqkv[:, sl + dc * 1024 + hp * 128:
                                       sl + dc * 1024 + (hp + 1) * 128],
                                  xnT[:, dc * T + tc * TCH:dc * T + (tc + 1) * TCH],
                                  start=(dc == 0), stop=(dc == 7))
                TS.inc(mm)
            for gh in range(16):
                gqi = 8 + gh
                pbank = psA if gqi % 2 == 0 else psB
                t.wait_ge(QG.h, qg_w(3 * tc + 1))
                prev = gqi - 2
                t.wait_ge(VS.h, (v_q(tc, prev) if prev < 8 else v_k(tc, prev - 8)))
                sl = ((3 * tc + 1) % 2) * 16384
                for dc in range(8):
                    mm = t.matmul(pbank[:, 0:TCH],
                                  wqkv[:, sl + dc * 2048 + gh * 128:
                                       sl + dc * 2048 + (gh + 1) * 128],
                                  xnT[:, dc * T + tc * TCH:dc * T + (tc + 1) * TCH],
                                  start=(dc == 0), stop=(dc == 7))
                TS.inc(mm)
            for gh in range(16):
                gqi = 24 + gh
                pbank = psA if gqi % 2 == 0 else psB
                t.wait_ge(QG.h, qg_w(3 * tc + 2))
                prev = gqi - 2
                t.wait_ge(VS.h, (v_k(tc, prev - 8) if prev < 24 else v_v(tc, prev - 24)))
                sl = ((3 * tc + 2) % 2) * 16384
                for dc in range(8):
                    mm = t.matmul(pbank[:, 0:TCH],
                                  wqkv[:, sl + dc * 2048 + gh * 128:
                                       sl + dc * 2048 + (gh + 1) * 128],
                                  xnT[:, dc * T + tc * TCH:dc * T + (tc + 1) * TCH],
                                  start=(dc == 0), stop=(dc == 7))
                TS.inc(mm)
            bl = {0: psC[:, 0:256], 1: psC[:, 512:768],
                  2: psD[:, 0:256], 3: psD[:, 512:768]}
            for gb in range(NB_E if LVL >= 2 else 0):
                t.wait_ge(VS.h, v_k(tc, 15))
                if gb >= 1:
                    t.wait_ge(VS.h, v_nm(tc, gb - 1))
                elif tc >= 1:
                    t.wait_ge(VS.h, v_nm(tc - 1, NB - 1))
                else:
                    t.wait_ge(VS.h, vs_b_end)
                for j in range(4):
                    G = 4 * gb + j
                    mm = t.matmul(bl[j][:, 0:128], qBD[:, G * 128:(G + 1) * 128],
                                  kBD[:, G * 128:(G + 1) * 128], start=True, stop=True)
                    TS.inc(mm)
                if LVL < 3:
                    continue
                t.wait_ge(SS.h, s_no(tc, gb))
                if gb >= 1:
                    t.wait_ge(VS.h, v_ae(tc, gb - 1))
                elif tc >= 1:
                    t.wait_ge(VS.h, v_ae(tc - 1, NB - 1))
                for j in range(4):
                    mm = t.transpose(psT[:, j * 128:(j + 1) * 128],
                                     attnb[:, j * 128:(j + 1) * 128], c_ident[:])
                TS.inc(mm)
                t.wait_ge(VS.h, v_v(tc, 15))
                for j in range(4):
                    G = 4 * gb + j
                    mm = t.transpose(psU[:, j * 128:(j + 1) * 128],
                                     vBDT[:, G * 128:(G + 1) * 128], c_ident[:])
                TS.inc(mm)
                if LVL < 4:
                    continue
                t.wait_ge(VS.h, v_ae(tc, gb))
                for j in range(4):
                    mm = t.matmul(bl[j][0:64, 128:256], vBDs[:, j * 64:(j + 1) * 64],
                                  attnT[:, j * 128:(j + 1) * 128], start=True, stop=True)
                TS.inc(mm)
            if LVL < 5:
                continue
            t.wait_ge(VS.h, v_sv(tc, NB - 1))
            t.wait_ge(QG.h, qg_c0 + 16)
            for m in range(1 if cstop == "outproj" else NM):
                gmt = tc * NM + m
                if gmt >= 1:
                    tcp, mp = divmod(gmt - 1, NM)
                    t.wait_ge(VS.h, v_fin(tcp, mp))
                for h in range(16):
                    lhs = svT[0:64].rearrange("q (t s) -> q t s", s=16)[
                        :, m * 128:(m + 1) * 128, h]
                    for nh in range(2):
                        pbank = psA if nh == 0 else psB
                        mm = t.matmul(pbank[:, 0:512], lhs,
                                      wpT[0:64, h * 1024 + nh * 512:
                                          h * 1024 + (nh + 1) * 512],
                                      start=(h == 0), stop=(h == 15))
                TS.inc(mm)
        if not cstop:
            assert TS.n == ts_c0 + NTC * TE_PER_TC

    @block.scalar
    def _(s):
        bl = {0: psC[:, 0:256], 1: psC[:, 512:768],
              2: psD[:, 0:256], 3: psD[:, 512:768]}
        for tc in range(NTC_E):
            for gb in range(NB_E if LVL >= 3 else 0):
                s.wait_ge(VS.h, v_nm(tc, gb))
                for j in range(4):
                    s.activation(expc[:, j * 128:(j + 1) * 128],
                                 expb[:, j * 128:(j + 1) * 128],
                                 AF.Exp, bias=nmax[:, j:j + 1],
                                 accum_out=sums[:, j:j + 1])
                SS.inc(s.drain())
                s.wait_ge(VS.h, v_rc(tc, gb))
                for j in range(4):
                    no = s.activation(attnb[:, j * 128:(j + 1) * 128],
                                      expc[:, j * 128:(j + 1) * 128],
                                      AF.Identity, scale=sums2[:, j:j + 1])
                SS.inc(no)
        if not cstop:
            assert SS.n == ss_c1 + NTC * S_PER_TC

    @block.vector
    def _(v):
        bl = {0: psC[:, 0:256], 1: psC[:, 512:768],
              2: psD[:, 0:256], 3: psD[:, 512:768]}
        v.memset(qBD[:], 0.0)
        v.drain()
        for tc in range(NTC_E):
            if tc >= 1:
                v.wait_ge(TS.h, t_sc(tc - 1, NB - 1, 3))
            for hp in range(8):
                pbank = psA if hp % 2 == 0 else psB
                v.wait_ge(TS.h, t_q(tc, hp))
                dst_lo = qBD[0:64].rearrange("q (t s) -> q t s", s=16)[:, :, 2 * hp]
                dst_hi = qBD[64:128].rearrange("q (t s) -> q t s", s=16)[:, :, 2 * hp + 1]
                v.tensor_scalar(dst_lo, pbank[0:64, 0:TCH], 0.125, None, ALU.mult)
                VS.inc(v.tensor_scalar(dst_hi, pbank[64:128, 0:TCH], 0.125, None,
                                       ALU.mult))
            for gh in range(16):
                pbank = psA if (8 + gh) % 2 == 0 else psB
                v.wait_ge(TS.h, t_k(tc, gh))
                dst = kBD[:].rearrange("q (t s) -> q t s", s=16)[:, :, gh]
                VS.inc(v.tensor_copy(dst, pbank[:, 0:TCH]))
            for gh in range(16):
                pbank = psA if (24 + gh) % 2 == 0 else psB
                v.wait_ge(TS.h, t_v(tc, gh))
                dst = vBDT[:].rearrange("q (t s) -> q t s", s=16)[:, :, gh]
                VS.inc(v.tensor_copy(dst, pbank[:, 0:TCH]))
            for gb in range(NB_E if LVL >= 2 else 0):
                v.wait_ge(TS.h, t_sc(tc, gb, 3))
                if tc == 0 and gb == 0:
                    v.wait_ge(QS.h, qs_spill_end + 48)
                for j in range(4):
                    v.tensor_tensor(expb[:, j * 128:(j + 1) * 128],
                                    bl[j][:, 0:128], c_bdmask[:], ALU.mult)
                    v.tensor_tensor(expb[:, j * 128:(j + 1) * 128],
                                    expb[:, j * 128:(j + 1) * 128],
                                    c_bdneg[:], ALU.add)
                for j in range(4):
                    nm = v.tensor_reduce(nmax[:, j:j + 1],
                                         expb[:, j * 128:(j + 1) * 128], AX.X,
                                         ALU.max, negate=True)
                VS.inc(nm)
                if LVL < 3:
                    continue
                v.wait_ge(SS.h, s_exp(tc, gb))
                VS.inc(v.reciprocal(sums2[:], sums[:]))
                v.wait_ge(TS.h, t_vt(tc, gb))
                for j in range(4):
                    v.tensor_copy(attnT[:, j * 128:(j + 1) * 128],
                                  psT[:, j * 128:(j + 1) * 128])
                for j in range(4):
                    ae = v.tensor_copy(vBDs[:, j * 64:(j + 1) * 64],
                                       psU[:, j * 128:j * 128 + 64])
                VS.inc(ae)
                if LVL < 4:
                    continue
                v.wait_ge(TS.h, t_av(tc, gb))
                for j in range(4):
                    G = 4 * gb + j
                    sv = v.tensor_copy(svT[0:64, G * 128:(G + 1) * 128],
                                       bl[j][0:64, 128:256])
                VS.inc(sv)
            for m in range(NM if LVL >= 5 and
                           (cstop is None or cstop.startswith("tc")) else 0):
                gmt = tc * NM + m
                v.wait_ge(TS.h, t_op(tc, m))
                v.wait_ge(QS.h, qs_load[gmt])
                if gmt >= 1:
                    v.wait_ge(QS.h, qs_store[gmt - 1])
                v.tensor_tensor(obuf[:, 0:512], psA[:, 0:512],
                                w_bprojB[:, 0:512], ALU.add)
                v.tensor_tensor(obuf[:, 512:1024], psB[:, 0:512],
                                w_bprojB[:, 512:1024], ALU.add)
                v.tensor_tensor(obuf[:], obuf[:], gbuf[:], ALU.mult)
                VS.inc(v.tensor_tensor(obuf[:], obuf[:], xbuf[:], ALU.add))
        if not cstop:
            assert VS.n == vs_c0 + NTC * V_PER_TC

    cscope.close()
    stack.close()
    return nc, d_dbg


def _prep_inputs(L0_attn_x, L0_attn_Wqkv, L0_attn_Wproj, L0_attn_bproj,
                 L0_ffn_proj_w, L0_ffn_proj_b, L0_ffn_expert_bias,
                 L0_ffn_fc1_w, L0_ffn_fc1_b, L0_ffn_fc2_w, L0_ffn_fc2_b,
                 L0_ffn_sh1_w, L0_ffn_sh1_b, L0_ffn_sh2_w, L0_ffn_sh2_b):
    f32 = np.float32
    x2d = np.ascontiguousarray(np.asarray(L0_attn_x, f32).reshape(NTOK, D))
    xT = np.ascontiguousarray(x2d.T)
    Wf = np.asarray(L0_attn_Wqkv, f32).reshape(16, 192, D)
    WqT = np.ascontiguousarray(Wf[:, 0:64].reshape(1024, D).T)
    WkT = np.ascontiguousarray(Wf[:, 64:128].reshape(1024, D).T)
    WvT = np.ascontiguousarray(Wf[:, 128:192].reshape(1024, D).T)

    def dup(WT):
        out = np.empty((D, 2048), f32)
        for h in range(16):
            out[:, 128 * h:128 * h + 64] = WT[:, 64 * h:64 * h + 64]
            out[:, 128 * h + 64:128 * h + 128] = WT[:, 64 * h:64 * h + 64]
        return np.ascontiguousarray(out)

    wkTd = dup(WkT)
    wvTd = dup(WvT)
    WpT = np.asarray(L0_attn_Wproj, f32).T
    wpT64 = np.ascontiguousarray(
        WpT.reshape(16, 64, D).transpose(1, 0, 2).reshape(64, 16 * D))
    bprojB = np.ascontiguousarray(
        np.broadcast_to(np.asarray(L0_attn_bproj, f32), (128, D)))
    fc1T = np.ascontiguousarray(np.asarray(L0_ffn_fc1_w, f32).transpose(0, 2, 1))
    fc2T = np.ascontiguousarray(np.asarray(L0_ffn_fc2_w, f32).transpose(0, 2, 1))
    sh1T = np.ascontiguousarray(np.asarray(L0_ffn_sh1_w, f32).T)
    sh2T = np.ascontiguousarray(np.asarray(L0_ffn_sh2_w, f32).T)
    fc1bP = np.ascontiguousarray(
        np.asarray(L0_ffn_fc1_b, f32).reshape(E, 16, 128)
        .transpose(2, 0, 1).reshape(128, E * 16))
    sh1bP = np.ascontiguousarray(np.asarray(L0_ffn_sh1_b, f32).reshape(32, 128).T)
    fc2bA = np.ascontiguousarray(np.asarray(L0_ffn_fc2_b, f32))
    sh2bB = np.ascontiguousarray(
        np.broadcast_to(np.asarray(L0_ffn_sh2_b, f32), (128, D)))
    pw9 = np.ascontiguousarray(
        np.concatenate([np.ones((D, 1), f32), np.asarray(L0_ffn_proj_w, f32).T], 1))
    ebP = np.ascontiguousarray(
        np.broadcast_to(np.asarray(L0_ffn_expert_bias, f32), (128, E)))
    pbP = np.ascontiguousarray(
        np.broadcast_to(np.asarray(L0_ffn_proj_b, f32), (128, E)))

    cones = np.ones((128, 128), f32)
    clt = np.ascontiguousarray(np.tril(np.ones((128, 128), f32), k=-1).T)
    cident = np.eye(128, dtype=f32)
    cbdmask = np.zeros((128, 128), f32)
    for jj in range(8):
        cbdmask[16 * jj:16 * jj + 16, 16 * jj:16 * jj + 16] = 1.0
    ciotaE = np.ascontiguousarray(
        np.repeat(np.arange(E, dtype=f32), NR)[None, :].repeat(128, 0))
    ce8 = np.zeros((8, E * 128), f32)
    for e in range(E):
        ce8[e, e * 128:(e + 1) * 128] = 1.0
    cio16 = np.broadcast_to((np.arange(NR) // NT).astype(f32), (128, NR)).copy()
    bisc = np.broadcast_to(np.array([2.0 ** (-k) for k in range(40)] + [float(CAP)],
                                    f32), (128, 41)).copy()
    epsB = np.full((128, 1), 1e-5, f32)

    shared = dict(xT=xT, wqT=WqT, wkTd=wkTd, wvTd=wvTd, wpT64=wpT64,
                  bprojB=bprojB, fc1T=fc1T, fc2T=fc2T, sh1T=sh1T, sh2T=sh2T,
                  fc1bP=fc1bP, sh1bP=sh1bP, fc2bA=fc2bA, sh2bB=sh2bB,
                  pw9=pw9, ebP=ebP, pbP=pbP, cones=cones, clt=clt,
                  cident=cident, cbdmask=cbdmask,
                  cbdneg=(cbdmask - 1.0) * 30000.0, ciotaE=ciotaE, ce8=ce8,
                  cio16=cio16, bisc=bisc, epsB=epsB)
    in_maps = []
    for c in range(NCORE):
        m = dict(shared)
        m["xL"] = np.ascontiguousarray(x2d[c * T:(c + 1) * T])
        m["cid"] = np.full((128, 1), float(c), f32)
        in_maps.append(m)
    return in_maps


def kernel(**inputs):
    in_maps = _prep_inputs(**inputs)
    nc, _ = build_program()
    res = run_bass_kernel_spmd(nc, in_maps, core_ids=list(range(NCORE)))
    out = np.concatenate([np.asarray(r["y"]) for r in res.results], axis=0)
    return np.ascontiguousarray(out.reshape(4, 4096, D).astype(np.float32))


# version 07
# speedup vs baseline: 1.0213x; 1.0213x over previous
import numpy as np
import ml_dtypes
from contextlib import ExitStack

import concourse.bass as bass
import concourse.mybir as mybir
from concourse.bass_utils import run_bass_kernel_spmd

F32 = mybir.dt.float32
BF16 = mybir.dt.bfloat16
U8 = mybir.dt.uint8
AF = mybir.ActivationFunctionType
ALU = mybir.AluOpType
AX = mybir.AxisListType
BF = ml_dtypes.bfloat16

NCORE = 8
NTOK = 16384
T = 2048
D = 1024
E = 8
EHE = 2048
SHH = 4096
CAP = 5120
NT = T // 128
NR = NTOK // 128
NITER = 50
NEG = -1.0e30


class Sem:
    def __init__(self, stack, nc, name):
        self.h = stack.enter_context(nc.semaphore(name))
        self.n = 0

    def inc(self, inst, k=1):
        inst.then_inc(self.h, k)
        self.n += k
        return self.n


def build_program(debug=(), stop_after=None, cstop=None, b_units=None, kc_rev=False,
                  fc2_serial=False, fc2_nh_outer=False, slot_shift=False,
                  fc2_dup0=False):
    dbg = set(debug)
    nc = bass.Bass("TRN2", target_bir_lowering=False)

    din = {}

    def I(name, shape, dt_=F32):
        din[name] = nc.dram_tensor(name, shape, dt_, kind="ExternalInput").ap()
        return din[name]

    d_xL = I("xL", [T, D])
    d_xT = I("xT", [D, NTOK])
    d_wqT = I("wqT", [D, 1024])
    d_wkTd = I("wkTd", [D, 2048])
    d_wvTd = I("wvTd", [D, 2048])
    d_wpT = I("wpT64", [64, 16 * 1024])
    d_bprojB = I("bprojB", [128, D])
    d_fc1T = I("fc1T", [E, D, EHE])
    d_fc2T = I("fc2T", [E, EHE, D])
    d_sh1T = I("sh1T", [D, SHH])
    d_sh2T = I("sh2T", [SHH, D])
    d_fc1bP = I("fc1bP", [128, E * 16])
    d_sh1bP = I("sh1bP", [128, 32])
    d_fc2bA = I("fc2bA", [E, D])
    d_sh2bB = I("sh2bB", [128, D])
    d_pw9 = I("pw9", [D, 9])
    d_ebP = I("ebP", [128, E])
    d_pbP = I("pbP", [128, E])
    d_cid = I("cid", [128, 1])
    d_bisc = I("bisc", [128, 41])
    d_epsB = I("epsB", [128, 1])
    d_cio16 = I("cio16", [128, NR])
    d_ones = I("cones", [128, 128])
    d_lt = I("clt", [128, 128])
    d_ident = I("cident", [128, 128])
    d_bdmask = I("cbdmask", [128, 128])
    d_bdneg = I("cbdneg", [128, 128])
    d_iotaE = I("ciotaE", [128, E * NR])
    d_e8 = I("ce8", [8, E * 128])

    d_y = nc.dram_tensor("y", [T, D], F32, kind="ExternalOutput").ap()
    d_gsp = nc.dram_tensor("gspill", [T, D], BF16, kind="Internal").ap()

    d_dbg = {}
    for name, shape in (("z", [128, E * NR]), ("w1", [128, NR]), ("w2", [128, NR]),
                        ("e1", [128, NR]), ("e2", [128, NR]), ("V", [128, E]),
                        ("r", [128, E]), ("g1", [128, NR]), ("g2", [128, NR]),
                        ("gates", [128, E * NT]), ("W", [128, E * NR]),
                        ("m1", [128, NR]), ("m2", [128, NR]),
                        ("acc2", [128, NT * D]), ("hbuf", [128, 16 * T]),
                        ("wf2", [128, 16 * D]),
                        ("xnT", [128, 8 * T]), ("svT", [64, 16 * 512]),
                        ("qBD", [128, 16 * 512]), ("kBD", [128, 16 * 512])):
        if name in dbg:
            d_dbg[name] = nc.dram_tensor("dbg_" + name, shape, F32,
                                         kind="ExternalOutput").ap()

    stack = ExitStack()
    sb = lambda n, s, dt_, **kw: stack.enter_context(nc.sbuf_tensor(n, s, dt_, **kw))
    ps = lambda n, s, dt_: stack.enter_context(nc.psum_tensor(n, s, dt_))

    psA = ps("psA", [128, 512], F32)
    psB = ps("psB", [128, 512], F32)
    psC = ps("psC", [128, 1024], F32)
    psD = ps("psD", [128, 1024], F32)
    psT = ps("psT", [128, 512], BF16)
    psU = ps("psU", [128, 512], BF16)

    xnT = sb("xnT", [128, 8 * T], BF16)
    gates = sb("gates", [128, E * NT], F32)
    c_ones = sb("c_ones", [128, 128], BF16)
    c_lt = sb("c_lt", [128, 128], BF16)
    c_ident = sb("c_ident", [128, 128], BF16)
    c_e8 = sb("c_e8", [8, E * 128], BF16)
    w_fc1bP = sb("w_fc1bP", [128, E * 16], F32)
    w_sh1bP = sb("w_sh1bP", [128, 32], F32)
    w_fc2bA = sb("w_fc2bA", [8, D], BF16)
    w_pw9 = sb("w_pw9", [128, 8 * 9], BF16)
    w_ebP = sb("w_ebP", [128, E], F32)
    w_pbP = sb("w_pbP", [128, E], F32)
    w_cid = sb("w_cid", [128, 1], F32)
    w_bisc = sb("w_bisc", [128, 41], F32)
    w_epsB = sb("w_epsB", [128, 1], F32)
    c_io16 = sb("c_io16", [128, NR], F32)

    QS = Sem(stack, nc, "qs")
    QG = Sem(stack, nc, "qg")
    TS = Sem(stack, nc, "ts")
    VS = Sem(stack, nc, "vs")
    SS = Sem(stack, nc, "ss")

    block = stack.enter_context(nc.Block())

    ascope = ExitStack()
    asb = lambda n, s, dt_, **kw: ascope.enter_context(nc.sbuf_tensor(n, s, dt_, **kw))
    ln_x = asb("ln_x", [128, 2 * D], F32)
    ln_sq = asb("ln_sq", [128, D], F32)
    ln_st = asb("ln_st", [128, 24], F32)
    ln_xn = asb("ln_xn", [128, 2 * D], BF16)
    xtc = asb("xtc", [128, 2 * 8 * 1024], BF16)
    xsq = asb("xsq", [128, 8 * 1024], BF16)
    zraw = asb("zraw", [128, 16 * 80], F32)
    r_z = asb("r_z", [128, E * NR], F32)
    r_a = asb("r_a", [128, E * NR], F32)
    r_b = asb("r_b", [128, E * NR], F32)
    r_eqb = asb("r_eqb", [128, E * NR], BF16)
    r_u8 = asb("r_u8", [128, E * NR], U8)
    r_io = asb("r_io", [128, E * NR], F32)
    r_m1 = asb("r_m1", [128, NR], F32)
    r_m2 = asb("r_m2", [128, NR], F32)
    r_e1 = asb("r_e1", [128, NR], F32)
    r_e2 = asb("r_e2", [128, NR], F32)
    r_w1 = asb("r_w1", [128, NR], F32)
    r_w2 = asb("r_w2", [128, NR], F32)
    r_t1 = asb("r_t1", [128, NR], F32)
    r_t2 = asb("r_t2", [128, NR], F32)
    r_t3 = asb("r_t3", [128, NR], F32)
    r_g1 = asb("r_g1", [128, NR], F32)
    r_g2 = asb("r_g2", [128, NR], F32)
    r_sw = asb("r_sw", [128, E], F32)
    r_midt = asb("r_midt", [128, E], F32)
    r_loS = asb("r_loS", [128, E], F32)
    r_hiS = asb("r_hiS", [128, E], F32)
    r_rrS = asb("r_rrS", [128, E], F32)
    r_cnt = asb("r_cnt", [128, E], BF16)
    r_bit = asb("r_bit", [128, E], BF16)
    r_ve = asb("r_ve", [128, NR], F32)
    r_re = asb("r_re", [128, NR], F32)
    r_rk = asb("r_rk", [128, NR], F32)
    r_ga = asb("r_ga", [128, E * NR], F32)

    @block.sync
    def _(sync):
        for dst, src in ((w_fc1bP[:], d_fc1bP[:]), (w_sh1bP[:], d_sh1bP[:]),
                         (w_ebP[:], d_ebP[:]), (w_pbP[:], d_pbP[:]),
                         (w_cid[:], d_cid[:]), (w_bisc[:], d_bisc[:]),
                         (w_epsB[:], d_epsB[:]),
                         (c_io16[:], d_cio16[:]),
                         (r_io[:], d_iotaE[:])):
            QS.inc(sync.dma_start(out=dst, in_=src), 16)
    m_sconsts = QS.n

    @block.gpsimd
    def _(g):
        for dst, src in ((c_ones[:], d_ones[:]), (c_lt[:], d_lt[:]),
                         (c_ident[:], d_ident[:]), (c_e8[:], d_e8[:]),
                         (w_fc2bA[:], d_fc2bA[:]),
                         (w_pw9[:].rearrange("p (c n) -> p c n", c=8),
                          d_pw9.rearrange("(c p) n -> p c n", p=128))):
            QG.inc(g.dma_start(out=dst, in_=src), 16)
    m_gconsts = QG.n

    qs0, ss0, vs0, ts0 = QS.n, SS.n, VS.n, TS.n
    xLr = d_xL.rearrange("(n p) d -> n p d", p=128)

    @block.sync
    def _(sync):
        for i in range(NT):
            if i >= 2:
                sync.wait_ge(SS.h, ss0 + 3 * (i - 2) + 3)
            QS.inc(sync.dma_start(out=ln_x[:, (i % 2) * D:(i % 2 + 1) * D],
                                  in_=xLr[i]), 16)
    assert QS.n == qs0 + 16 * NT

    @block.scalar
    def _(s):
        for i in range(NT):
            r = i % 2
            b = 12 * r
            muraw = ln_st[:, b + 0:b + 1]
            ssc = ln_st[:, b + 1:b + 2]
            muc = ln_st[:, b + 2:b + 3]
            tA = ln_st[:, b + 3:b + 4]
            tB = ln_st[:, b + 4:b + 5]
            var = ln_st[:, b + 5:b + 6]
            vare = ln_st[:, b + 6:b + 7]
            std = ln_st[:, b + 7:b + 8]
            rstd = ln_st[:, b + 8:b + 9]
            t2 = ln_st[:, b + 9:b + 10]
            nmr = ln_st[:, b + 10:b + 11]
            s.wait_ge(QS.h, qs0 + 16 * (i + 1))
            s.activation(ln_sq[:], ln_x[:, r * D:(r + 1) * D], AF.Square,
                         accum_out=ssc)
            SS.inc(s.drain())
            s.wait_ge(VS.h, vs0 + 3 * i + 1)
            s.activation(muc, muraw, AF.Identity, scale=1.0 / D)
            s.drain()
            s.activation(tA, muc, AF.Square)
            s.drain()
            s.activation(tB, tA, AF.Identity, scale=-1.0)
            s.drain()
            s.activation(var, ssc, AF.Identity, scale=1.0 / D, bias=tB)
            s.drain()
            s.activation(vare, var, AF.Identity, bias=w_epsB[:, 0:1])
            s.drain()
            SS.inc(s.activation(std, vare, AF.Sqrt))
            s.wait_ge(VS.h, vs0 + 3 * i + 2)
            s.activation(t2, muc, AF.Identity, scale=rstd)
            s.drain()
            s.activation(nmr, t2, AF.Identity, scale=-1.0)
            s.drain()
            SS.inc(s.activation(ln_xn[:, r * D:(r + 1) * D],
                                ln_x[:, r * D:(r + 1) * D], AF.Identity,
                                scale=rstd, bias=nmr))
    assert SS.n == ss0 + 3 * NT

    @block.vector
    def _(v):
        for i in range(NT):
            r = i % 2
            b = 12 * r
            v.wait_ge(QS.h, qs0 + 16 * (i + 1))
            VS.inc(v.tensor_reduce(ln_st[:, b + 0:b + 1],
                                   ln_x[:, r * D:(r + 1) * D], AX.X, ALU.add))
            v.wait_ge(SS.h, ss0 + 3 * i + 2)
            VS.inc(v.reciprocal(ln_st[:, b + 8:b + 9], ln_st[:, b + 7:b + 8]))
            v.wait_ge(TS.h, ts0 + i + 1)
            for c in range(8):
                srcp = (psT[:, (c % 4) * 128:(c % 4 + 1) * 128] if c < 4
                        else psU[:, (c - 4) * 128:(c - 3) * 128])
                ev = v.tensor_copy(xnT[:, c * T + i * 128:c * T + (i + 1) * 128], srcp)
            VS.inc(ev)
    assert VS.n == vs0 + 3 * NT

    @block.tensor
    def _(t):
        for i in range(NT):
            r = i % 2
            t.wait_ge(SS.h, ss0 + 3 * i + 3)
            if i >= 1:
                t.wait_ge(VS.h, vs0 + 3 * (i - 1) + 3)
            for c in range(8):
                dst = (psT[:, (c % 4) * 128:(c % 4 + 1) * 128] if c < 4
                       else psU[:, (c - 4) * 128:(c - 3) * 128])
                mm = t.transpose(dst, ln_xn[:, r * D + c * 128:r * D + (c + 1) * 128],
                                 c_ident[:])
            TS.inc(mm)
    assert TS.n == ts0 + NT
    m_xnT = VS.n

    qg0, ss1, ts1, vs1 = QG.n, SS.n, TS.n, VS.n
    ts_sw = ts1 + 1
    vs_sw = vs1 + 1
    xTr = d_xT.rearrange("(c p) n -> p c n", p=128)

    ss_sw = ss1 + 1

    def a2_sq(hc):
        return ss_sw + 2 * hc + 1

    def a2_ev(hc):
        return ss_sw + 2 * hc + 2

    @block.gpsimd
    def _(g):
        for hc in range(16):
            if hc >= 2:
                g.wait_ge(SS.h, a2_sq(hc - 2))
                g.wait_ge(TS.h, ts_sw + (hc - 2) + 1)
            r = hc % 2
            QG.inc(g.dma_start(
                out=xtc[:, r * 8192:(r + 1) * 8192].rearrange("p (c n) -> p c n", c=8),
                in_=xTr[:, :, hc * 1024:(hc + 1) * 1024]), 16)
    assert QG.n == qg0 + 16 * 16

    @block.scalar
    def _(s):
        s.wait_ge(TS.h, ts_sw)
        SS.inc(s.copy(r_sw[:], psB[:, 0:8]))
        assert SS.n == ss_sw
        for hc in range(16):
            r = hc % 2
            pbank = psA if hc % 2 == 0 else psB
            s.wait_ge(QG.h, qg0 + 16 * (hc + 1))
            if hc >= 1:
                s.wait_ge(TS.h, ts_sw + hc)
            SS.inc(s.activation(xsq[:], xtc[:, r * 8192:(r + 1) * 8192], AF.Square))
            s.wait_ge(TS.h, ts_sw + hc + 1)
            SS.inc(s.copy(zraw[:, hc * 80:(hc + 1) * 80], pbank[:, 0:80]))
    assert SS.n == ss_sw + 32

    @block.tensor
    def _(t):
        t.wait_ge(QG.h, m_gconsts)
        for dc in range(8):
            mm = t.matmul(psB[:, 0:8], c_ones[:], w_pw9[:, dc * 9 + 1:dc * 9 + 9],
                          start=(dc == 0), stop=(dc == 7))
        TS.inc(mm)
        for hc in range(16):
            r = hc % 2
            pbank = psA if hc % 2 == 0 else psB
            t.wait_ge(QG.h, qg0 + 16 * (hc + 1))
            t.wait_ge(SS.h, a2_sq(hc))
            if hc >= 2:
                t.wait_ge(SS.h, a2_ev(hc - 2))
            if hc == 1:
                t.wait_ge(SS.h, ss_sw)
            for i in range(8):
                for dc in range(8):
                    t.matmul(pbank[:, i * 9:i * 9 + 9],
                             xtc[:, r * 8192 + dc * 1024 + i * 128:
                                 r * 8192 + dc * 1024 + (i + 1) * 128],
                             w_pw9[:, dc * 9:(dc + 1) * 9],
                             start=(dc == 0), stop=(dc == 7))
                for dc in range(8):
                    mm = t.matmul(pbank[:, 72 + i:73 + i],
                                  xsq[:, dc * 1024 + i * 128:dc * 1024 + (i + 1) * 128],
                                  c_ones[:, 0:1],
                                  start=(dc == 0), stop=(dc == 7))
            TS.inc(mm)
    assert TS.n == ts_sw + 16
    ss_a2end = SS.n

    zr3 = zraw[:].rearrange("p (h s) -> p h s", h=16)
    zr4 = zr3[:, :, 0:72].rearrange("p h (i f) -> p h i f", f=9)
    muAP = zr4[:, :, :, 0]
    ssAP = zr3[:, :, 72:80]
    t1v = lambda t: t[:].rearrange("p (h i) -> p h i", h=16)

    @block.vector
    def _(v):
        v.wait_ge(SS.h, ss_a2end)
        v.drain()
        v.tensor_scalar(t1v(r_t1), muAP, 1.0 / D, None, ALU.mult)
        v.drain()
        v.tensor_scalar(t1v(r_t2), ssAP, 1.0 / D, None, ALU.mult)
        v.tensor_tensor(r_t3[:], r_t1[:], r_t1[:], ALU.mult)
        v.tensor_tensor(r_t2[:], r_t2[:], r_t3[:], ALU.subtract)
        v.tensor_scalar(r_t2[:], r_t2[:], 1e-5, None, ALU.add)
        VS.inc(v.reciprocal(r_t2[:], r_t2[:]))
    vs_var = VS.n

    @block.scalar
    def _(s):
        s.wait_ge(VS.h, vs_var)
        SS.inc(s.activation(r_t2[:], r_t2[:], AF.Sqrt))
    ss_rstd = SS.n

    @block.vector
    def _(v):
        v.wait_ge(SS.h, ss_rstd)
        v.wait_ge(QS.h, m_sconsts)
        for e in range(E):
            pe = zr4[:, :, :, 1 + e]
            zsl = r_z[:, e * NR:(e + 1) * NR]
            v.tensor_scalar(r_t3[:], r_t1[:], r_sw[:, e:e + 1], None, ALU.mult)
            v.drain()
            v.tensor_tensor(zsl.rearrange("p (h i) -> p h i", h=16), pe,
                            t1v(r_t3), ALU.subtract)
            v.drain()
            v.tensor_tensor(zsl, zsl, r_t2[:], ALU.mult)
            last = v.tensor_scalar(zsl, zsl, w_pbP[:, e:e + 1], None, ALU.add)
        VS.inc(last)
    vs_z = VS.n

    @block.scalar
    def _(s):
        s.wait_ge(VS.h, vs_z)
        SS.inc(s.activation(r_z[:], r_z[:], AF.Sigmoid))
    ss_sig = SS.n

    @block.vector
    def _(v):
        v.wait_ge(SS.h, ss_sig)
        v.wait_ge(QS.h, m_sconsts)
        for e in range(E):
            zsl = r_z[:, e * NR:(e + 1) * NR]
            v.tensor_scalar(zsl, zsl, w_ebP[:, e:e + 1], None, ALU.add)
        VS.inc(v.tensor_scalar(r_z[:], r_z[:], 1e-12, None, ALU.max))
    vs_clip = VS.n

    @block.scalar
    def _(s):
        s.wait_ge(VS.h, vs_clip)
        SS.inc(s.activation(r_z[:], r_z[:], AF.Ln))
    ss_ln = SS.n

    @block.vector
    def _(v):
        v.wait_ge(SS.h, ss_ln)
        v.tensor_tensor(r_a[:, 0:512], r_z[:, 0:512], r_z[:, 512:1024], ALU.max)
        v.tensor_tensor(r_a[:, 0:256], r_a[:, 0:256], r_a[:, 256:512], ALU.max)
        v.tensor_tensor(r_m1[:], r_a[:, 0:128], r_a[:, 128:256], ALU.max)
        for e in range(E):
            v.tensor_copy(r_b[:, e * NR:(e + 1) * NR], r_m1[:])
        v.drain()
        v.tensor_tensor(r_u8[:], r_z[:], r_b[:], ALU.is_equal)
        v.tensor_scalar(r_a[:], r_io[:], 0.0, 99.0, ALU.mult, ALU.add)
        v.select(r_a[:], r_u8[:], r_io[:], r_a[:])
        v.tensor_tensor(r_a[:, 0:512], r_a[:, 0:512], r_a[:, 512:1024], ALU.min)
        v.tensor_tensor(r_a[:, 0:256], r_a[:, 0:256], r_a[:, 256:512], ALU.min)
        v.tensor_tensor(r_e1[:], r_a[:, 0:128], r_a[:, 128:256], ALU.min)
        for e in range(E):
            v.tensor_copy(r_b[:, e * NR:(e + 1) * NR], r_e1[:])
        v.drain()
        v.tensor_tensor(r_a[:], r_io[:], r_b[:], ALU.is_equal)
        v.scalar_tensor_tensor(r_b[:], r_a[:], NEG, r_z[:], ALU.mult, ALU.add)
        v.tensor_tensor(r_a[:, 0:512], r_b[:, 0:512], r_b[:, 512:1024], ALU.max)
        v.tensor_tensor(r_a[:, 0:256], r_a[:, 0:256], r_a[:, 256:512], ALU.max)
        v.tensor_tensor(r_m2[:], r_a[:, 0:128], r_a[:, 128:256], ALU.max)
        for e in range(E):
            v.tensor_copy(r_a[:, e * NR:(e + 1) * NR], r_m2[:])
        v.drain()
        v.tensor_tensor(r_u8[:], r_b[:], r_a[:], ALU.is_equal)
        v.tensor_scalar(r_a[:], r_io[:], 0.0, 99.0, ALU.mult, ALU.add)
        v.select(r_a[:], r_u8[:], r_io[:], r_a[:])
        v.tensor_tensor(r_a[:, 0:512], r_a[:, 0:512], r_a[:, 512:1024], ALU.min)
        v.tensor_tensor(r_a[:, 0:256], r_a[:, 0:256], r_a[:, 256:512], ALU.min)
        v.tensor_tensor(r_e2[:], r_a[:, 0:128], r_a[:, 128:256], ALU.min)
        VS.inc(v.tensor_tensor(r_t1[:], r_m1[:], r_m2[:], ALU.subtract))
    vs_top = VS.n

    @block.scalar
    def _(s):
        s.wait_ge(VS.h, vs_top)
        s.activation(r_w1[:], r_t1[:], AF.Sigmoid)
        SS.inc(s.activation(r_w2[:], r_t1[:], AF.Sigmoid, scale=-1.0))
    ss_w = SS.n

    @block.vector
    def _(v):
        v.wait_ge(SS.h, ss_w)
        for e in range(E):
            fe = float(e)
            wsl = r_a[:, e * NR:(e + 1) * NR]
            v.tensor_scalar(r_t1[:], r_e1[:], fe, None, ALU.is_equal)
            v.tensor_scalar(r_t2[:], r_e2[:], fe, None, ALU.is_equal)
            v.tensor_tensor(r_t3[:], r_t1[:], r_w1[:], ALU.mult)
            v.tensor_tensor(wsl, r_t2[:], r_w2[:], ALU.mult)
            v.tensor_tensor(wsl, wsl, r_t3[:], ALU.add)
            v.tensor_tensor(r_t3[:], r_t1[:], r_t2[:], ALU.add)
            v.tensor_scalar(r_t3[:], r_t3[:], 1.0, None, ALU.subtract)
            last = v.tensor_tensor(wsl, wsl, r_t3[:], ALU.add)
        VS.inc(last)
    vs_wb = VS.n

    NBIS = 40
    DELTA = [2.0 ** (-k) for k in range(NBIS)]
    s_mid, v_cmp, t_cnt_m, v_bit, t_acc = {}, {}, {}, {}, {}
    scur, vcur, tcur = SS.n, VS.n, TS.n
    for k in range(NBIS):
        scur += 1; s_mid[k] = scur
        vcur += 1; v_cmp[k] = vcur
        tcur += 1; t_cnt_m[k] = tcur
        if k < NBIS - 0:
            vcur += 1; v_bit[k] = vcur
            tcur += 1; t_acc[k] = tcur
    scur += 1
    s_fin = scur
    vcur += 1; v_cmp[NBIS] = vcur
    tcur += 1; t_cnt_m[NBIS] = tcur
    scur += 1
    s_rr = scur

    @block.scalar
    def _(s):
        for k in range(NBIS):
            if k == 0:
                s.wait_ge(VS.h, vs_wb)
                s.wait_ge(QS.h, m_sconsts)
                SS.inc(s.activation(r_midt[:], w_ebP[:], AF.Identity,
                                    bias=w_bisc[:, 0:1], scale=0.0))
            else:
                s.wait_ge(TS.h, t_acc[k - 1])
                SS.inc(s.activation(r_midt[:], psA[:, 0:8], AF.Identity,
                                    bias=w_bisc[:, k:k + 1]))
        s.wait_ge(TS.h, t_acc[NBIS - 1])
        s.copy(r_loS[:], psA[:, 0:8])
        SS.inc(s.activation(r_hiS[:], psA[:, 0:8], AF.Identity,
                            bias=w_bisc[:, NBIS - 1:NBIS]))
        assert SS.n == s_fin
        s.wait_ge(TS.h, t_cnt_m[NBIS])
        SS.inc(s.activation(r_rrS[:], psB[:, 0:8], AF.Identity,
                            scale=-1.0, bias=w_bisc[:, NBIS:NBIS + 1]))
        assert SS.n == s_rr

    @block.vector
    def _(v):
        for k in range(NBIS + 1):
            v.wait_ge(SS.h, s_mid[k] if k < NBIS else s_fin)
            th = r_midt if k < NBIS else r_hiS
            for e in range(E):
                v.tensor_scalar(r_t1[:], r_a[:, e * NR:(e + 1) * NR],
                                th[:, e:e + 1], 0.0, ALU.is_ge, ALU.add,
                                accum_out=r_cnt[:, e:e + 1])
            VS.inc(v.drain())
            assert VS.n == v_cmp[k]
            if k < NBIS:
                v.wait_ge(TS.h, t_cnt_m[k])
                VS.inc(v.tensor_scalar(r_bit[:], psB[:, 0:8], float(CAP), DELTA[k],
                                       ALU.is_ge, ALU.mult))
                assert VS.n == v_bit[k]

    @block.tensor
    def _(t):
        for k in range(NBIS + 1):
            t.wait_ge(VS.h, v_cmp[k])
            TS.inc(t.matmul(psB[:, 0:8], c_ones[:], r_cnt[:], start=True, stop=True))
            assert TS.n == t_cnt_m[k]
            if k < NBIS:
                t.wait_ge(VS.h, v_bit[k])
                TS.inc(t.matmul(psA[:, 0:8], c_ident[:], r_bit[:],
                                start=(k == 0), stop=(k == NBIS - 1)))
                assert TS.n == t_acc[k]

    @block.vector
    def _(v):
        v.wait_ge(SS.h, s_fin)
        for e in range(E):
            v.tensor_scalar(r_t1[:], r_a[:, e * NR:(e + 1) * NR],
                            r_loS[:, e:e + 1], None, ALU.is_ge)
            v.tensor_scalar(r_t2[:], r_a[:, e * NR:(e + 1) * NR],
                            r_hiS[:, e:e + 1], None, ALU.is_lt)
            last = v.tensor_tensor(r_eqb[:, e * NR:(e + 1) * NR], r_t1[:], r_t2[:],
                                   ALU.mult)
        VS.inc(last)
    vs_eq = VS.n

    @block.tensor
    def _(t):
        t.wait_ge(VS.h, vs_eq)
        t.matmul(psC[:, 0:512], c_ones[:], r_eqb[:, 0:512], start=True, stop=True)
        t.matmul(psC[:, 512:1024], c_ones[:], r_eqb[:, 512:1024], start=True, stop=True)
        t.matmul(psD[:, 0:512], c_lt[:], r_eqb[:, 0:512], start=True, stop=True)
        TS.inc(t.matmul(psD[:, 512:1024], c_lt[:], r_eqb[:, 512:1024],
                        start=True, stop=True))
    ts_eq = TS.n

    @block.vector
    def _(v):
        v.wait_ge(TS.h, ts_eq)
        v.wait_ge(SS.h, s_rr)
        v.tensor_copy(r_b[:], psC[:])
        for e in range(E):
            sl = r_z[:, e * NR:(e + 1) * NR]
            v.memset(sl[:, 0:1], 0.0)
            v.tensor_copy(sl[:, 1:NR], r_b[:, e * NR:e * NR + NR - 1])
        v.drain()
        src_, dst_ = r_z, r_b
        for st in range(7):
            sh = 1 << st
            for e in range(E):
                ssl = src_[:, e * NR:(e + 1) * NR]
                dsl = dst_[:, e * NR:(e + 1) * NR]
                v.tensor_copy(dsl[:, 0:sh], ssl[:, 0:sh])
                v.tensor_tensor(dsl[:, sh:NR], ssl[:, sh:NR], ssl[:, 0:NR - sh], ALU.add)
            src_, dst_ = dst_, src_
            v.drain()
        rank = src_
        v.tensor_tensor(rank[:], rank[:], psD[:], ALU.add)
        for slot, (we, ee, gg) in enumerate(((r_w1, r_e1, r_g1), (r_w2, r_e2, r_g2))):
            v.memset(r_ve[:], 0.0)
            v.memset(r_re[:], 0.0)
            v.memset(r_rk[:], 0.0)
            v.memset(r_t3[:], 0.0)
            v.drain()
            for e in range(E):
                fe = float(e)
                v.tensor_scalar(r_t1[:], ee[:], fe, None, ALU.is_equal)
                v.scalar_tensor_tensor(r_t2[:], r_t1[:], r_loS[:, e:e + 1], r_ve[:],
                                       ALU.mult, ALU.add)
                v.tensor_copy(r_ve[:], r_t2[:])
                v.scalar_tensor_tensor(r_t2[:], r_t1[:], r_hiS[:, e:e + 1], r_t3[:],
                                       ALU.mult, ALU.add)
                v.tensor_copy(r_t3[:], r_t2[:])
                v.scalar_tensor_tensor(r_t2[:], r_t1[:], r_rrS[:, e:e + 1], r_re[:],
                                       ALU.mult, ALU.add)
                v.tensor_copy(r_re[:], r_t2[:])
                v.tensor_tensor(r_t2[:], r_t1[:], rank[:, e * NR:(e + 1) * NR], ALU.mult)
                v.tensor_tensor(r_rk[:], r_rk[:], r_t2[:], ALU.add)
            v.drain()
            v.tensor_tensor(r_t1[:], we[:], r_t3[:], ALU.is_ge)
            v.tensor_tensor(r_t2[:], we[:], r_ve[:], ALU.is_ge)
            v.tensor_tensor(r_t2[:], r_t2[:], r_t1[:], ALU.subtract)
            v.tensor_tensor(r_t3[:], r_rk[:], r_re[:], ALU.is_lt)
            v.tensor_tensor(r_t2[:], r_t2[:], r_t3[:], ALU.mult)
            v.tensor_tensor(r_t1[:], r_t1[:], r_t2[:], ALU.add)
            v.tensor_tensor(gg[:], we[:], r_t1[:], ALU.mult)
            v.drain()
        for e in range(E):
            fe = float(e)
            gsl = r_ga[:, e * NR:(e + 1) * NR]
            v.tensor_scalar(r_t1[:], r_e1[:], fe, None, ALU.is_equal)
            v.tensor_tensor(gsl, r_t1[:], r_g1[:], ALU.mult)
            v.tensor_scalar(r_t1[:], r_e2[:], fe, None, ALU.is_equal)
            v.tensor_tensor(r_t2[:], r_t1[:], r_g2[:], ALU.mult)
            v.tensor_tensor(gsl, gsl, r_t2[:], ALU.add)
        v.drain()
        v.tensor_scalar(r_t1[:], c_io16[:], w_cid[:, 0:1], None, ALU.is_equal)
        for e in range(E):
            v.tensor_tensor(r_t2[:], r_ga[:, e * NR:(e + 1) * NR], r_t1[:], ALU.mult)
            v.drain()
            src3 = r_t2[:].rearrange("p (c t) -> p c t", c=8).transpose([0, 2, 1])
            dst3 = gates[:, e * NT:(e + 1) * NT].rearrange("p (t o) -> p t o", o=1)
            last = v.tensor_reduce(dst3, src3, AX.X, ALU.add)
            v.drain()
        VS.inc(last)
    vs_gates = VS.n

    if d_dbg:
        dbg_map = {"z": r_z, "w1": r_w1, "w2": r_w2, "e1": r_e1, "e2": r_e2,
                   "V": r_loS, "r": r_rrS, "g1": r_g1, "g2": r_g2, "gates": gates,
                   "W": r_a, "m1": r_m1, "m2": r_m2, "xnT": xnT}

        @block.sync
        def _(sync):
            sync.wait_ge(VS.h, vs_gates)
            for name in d_dbg:
                if name in dbg_map and dbg_map[name].ap().dtype == F32:
                    QS.inc(sync.dma_start(out=d_dbg[name][:], in_=dbg_map[name][:]), 16)

        @block.gpsimd
        def _(g):
            g.wait_ge(VS.h, vs_gates)
            for name in d_dbg:
                if name in dbg_map and dbg_map[name].ap().dtype != F32:
                    QG.inc(g.dma_start(out=d_dbg[name][:], in_=dbg_map[name][:]), 16)

    ascope.close()
    if stop_after == "A":
        stack.close()
        return nc, d_dbg

    accscope = ExitStack()
    acc = accscope.enter_context(nc.sbuf_tensor("acc", [128, NT * D], F32))
    bscope = ExitStack()
    bsb = lambda n, s, dt_, **kw: bscope.enter_context(nc.sbuf_tensor(n, s, dt_, **kw))
    hbuf = bsb("hbuf", [128, 16 * T], BF16)
    wf1 = bsb("wf1", [128, 2 * 1024], BF16)
    wf2 = bsb("wf2", [128, 16 * D], BF16)
    w_sh2bB = bsb("w_sh2bB", [128, D], F32)

    units = [("sq", q) for q in range(4)] + [("ex", e) for e in range(E)]
    if b_units is not None:
        units = units[:b_units]

    def udims(u):
        kind, idx = units[u]
        ehu = 1024 if kind == "sq" else EHE
        return kind, idx, ehu, ehu // 128

    ts_b0, ss_b0, vs_b0, qg_b0 = TS.n, SS.n, VS.n, QG.n
    unit_ts = []
    tc_ = ts_b0
    for u in range(len(units)):
        _, _, _, nsl = udims(u)
        unit_ts.append(tc_)
        tc_ += nsl * 4 + 16
    ts_b_end = tc_
    gmap = []
    for u in range(len(units)):
        _, _, _, nsl = udims(u)
        for gi in range(nsl * 4):
            gmap.append((u, gi))
    NGRP = len(gmap)
    ss_of_group = {}
    cnt = ss_b0
    for g_ in range(NGRP):
        cnt += 1
        ss_of_group[g_] = cnt
    ss_b_end = cnt
    ts_of_group = {}
    for g_, (u, gi) in enumerate(gmap):
        ts_of_group[g_] = unit_ts[u] + gi + 1
    vs_of_gt = {gt: vs_b0 + gt + 1 for gt in range(16 * len(units))}
    vs_b_end = vs_b0 + 16 * len(units)
    smap = []
    qg_of_unit_fc2 = {}
    qg_of_slice = {}
    qq = qg_b0
    for u in range(len(units)):
        _, _, _, nsl = udims(u)
        qq += 16
        qg_of_slice[len(smap)] = qq
        smap.append((u, 0))
        qq += 16
        qg_of_slice[len(smap)] = qq
        smap.append((u, 1))
        qq += 16
        qg_of_unit_fc2[u] = qq
        for sl in range(2, nsl):
            qq += 16
            qg_of_slice[len(smap)] = qq
            smap.append((u, sl))
    qg_b_end = qq
    ts_slice_done = {i: unit_ts[u] + (sl + 1) * 4 for i, (u, sl) in enumerate(smap)}

    @block.sync
    def _(sync):
        QS.inc(sync.dma_start(out=w_sh2bB[:], in_=d_sh2bB[:]), 16)
    qs_sh2b = QS.n

    @block.gpsimd
    def _(g):
        sno = 0

        def load_slice(kind, idx, sl):
            nonlocal sno
            if sno >= 2:
                g.wait_ge(TS.h, ts_slice_done[sno - 2])
            rr = sno % 2
            if kind == "ex":
                src1 = d_fc1T[idx][:, sl * 128:(sl + 1) * 128].rearrange(
                    "(c p) h -> p c h", p=128)
            else:
                src1 = d_sh1T[:, idx * 1024 + sl * 128:idx * 1024 + (sl + 1) * 128
                              ].rearrange("(c p) h -> p c h", p=128)
            QG.inc(g.dma_start(
                out=wf1[:, rr * 1024:(rr + 1) * 1024].rearrange(
                    "p (c h) -> p c h", c=8),
                in_=src1), 16)
            sno += 1

        for u in range(len(units)):
            kind, idx, ehu, nsl = udims(u)
            load_slice(kind, idx, 0)
            load_slice(kind, idx, 1)
            if kind == "ex":
                if u >= 1:
                    g.wait_ge(TS.h, unit_ts[u])
                src2 = d_fc2T[idx].rearrange("(k p) d -> p k d", p=128)
                QG.inc(g.dma_start(
                    out=wf2[:, 0:16 * D].rearrange("p (k d) -> p k d", k=16),
                    in_=src2), 16)
            else:
                off = (idx % 2) * 8 * D
                if u >= 2:
                    g.wait_ge(TS.h, unit_ts[u - 1])
                src2 = d_sh2T[idx * 1024:(idx + 1) * 1024].rearrange(
                    "(k p) d -> p k d", p=128)
                QG.inc(g.dma_start(
                    out=wf2[:, off:off + 8 * D].rearrange("p (k d) -> p k d", k=8),
                    in_=src2), 16)
            for sl in range(2, nsl):
                load_slice(kind, idx, sl)
    assert QG.n == qg_b_end

    @block.tensor
    def _(t):
        t.wait_ge(VS.h, vs_gates)
        gglob = 0
        sno = 0
        for u in range(len(units)):
            kind, idx, ehu, nsl = udims(u)
            nkc = ehu // 128
            for sl in range(nsl):
                t.wait_ge(QG.h, qg_of_slice[sno])
                for tcx in range(4):
                    pbank = psA if (gglob % 2 == 0) else psB
                    if gglob >= 2:
                        t.wait_ge(SS.h, ss_of_group[gglob - 2])
                    rr = sno % 2
                    for dc in range(8):
                        mm = t.matmul(
                            pbank[:, 0:512],
                            wf1[:, rr * 1024 + dc * 128:rr * 1024 + (dc + 1) * 128],
                            xnT[:, dc * T + tcx * 512:dc * T + (tcx + 1) * 512],
                            start=(dc == 0), stop=(dc == 7))
                    TS.inc(mm)
                    gglob += 1
                sno += 1
            t.wait_ge(SS.h, ss_of_group[gglob - 1])
            t.wait_ge(QG.h, qg_of_unit_fc2[u])
            wof = (idx % 2) * 8 * D if kind == "sq" else 0
            for tile in range(16):
                gt = 16 * u + tile
                pbank = psC if (gt % 2 == 0) else psD
                if fc2_serial:
                    if gt >= 1:
                        t.wait_ge(VS.h, vs_of_gt[gt - 1])
                elif gt >= 2:
                    t.wait_ge(VS.h, vs_of_gt[gt - 2])
                kcs = list(range(nkc))[::-1] if kc_rev else list(range(nkc))
                if fc2_nh_outer:
                    for nh in range(2):
                        for ki, kc in enumerate(kcs):
                            laststop = (ki == nkc - 1) and (kind != "ex")
                            mm = t.matmul(
                                pbank[:, nh * 512:(nh + 1) * 512],
                                hbuf[:, kc * T + tile * 128:kc * T + (tile + 1) * 128],
                                wf2[:, wof + kc * D + nh * 512:wof + kc * D + (nh + 1) * 512],
                                start=(ki == 0), stop=laststop)
                        if kind == "ex":
                            mm = t.matmul(pbank[:, nh * 512:(nh + 1) * 512],
                                          c_e8[:, idx * 128:(idx + 1) * 128],
                                          w_fc2bA[:, nh * 512:(nh + 1) * 512],
                                          start=False, stop=True)
                else:
                    for ki, kc in enumerate(kcs):
                        laststop = (ki == nkc - 1) and (kind != "ex")
                        kslot = ((kc + 1) % nkc) if slot_shift else kc
                        for nh in range(2):
                            if fc2_dup0 and ki == 0:
                                t.matmul(
                                    pbank[:, nh * 512:(nh + 1) * 512],
                                    hbuf[:, kslot * T + tile * 128:kslot * T + (tile + 1) * 128],
                                    wf2[:, wof + kc * D + nh * 512:wof + kc * D + (nh + 1) * 512],
                                    start=True, stop=False)
                            mm = t.matmul(
                                pbank[:, nh * 512:(nh + 1) * 512],
                                hbuf[:, kslot * T + tile * 128:kslot * T + (tile + 1) * 128],
                                wf2[:, wof + kc * D + nh * 512:wof + kc * D + (nh + 1) * 512],
                                start=(ki == 0), stop=laststop)
                    if kind == "ex":
                        for nh in range(2):
                            mm = t.matmul(pbank[:, nh * 512:(nh + 1) * 512],
                                          c_e8[:, idx * 128:(idx + 1) * 128],
                                          w_fc2bA[:, nh * 512:(nh + 1) * 512],
                                          start=False, stop=True)
                TS.inc(mm)
        assert TS.n == ts_b_end

    @block.scalar
    def _(s):
        for g_, (u, gi) in enumerate(gmap):
            kind, idx, ehu, nsl = udims(u)
            pbank = psA if (g_ % 2 == 0) else psB
            kglob = gi // 4
            tcx = gi % 4
            if gi == 0 and u >= 1:
                s.wait_ge(TS.h, unit_ts[u])
            s.wait_ge(TS.h, ts_of_group[g_])
            bias = (w_fc1bP[:, idx * 16 + kglob:idx * 16 + kglob + 1]
                    if kind == "ex" else
                    w_sh1bP[:, idx * 8 + kglob:idx * 8 + kglob + 1])
            kslot = ((kglob + 1) % nsl) if slot_shift else kglob
            SS.inc(s.activation(
                hbuf[:, kslot * T + tcx * 512:kslot * T + (tcx + 1) * 512],
                pbank[:, 0:512], AF.Gelu, bias=bias))
        assert SS.n == ss_b_end

    @block.vector
    def _(v):
        for u in range(len(units)):
            kind, idx, ehu, nsl = udims(u)
            for tile in range(16):
                gt = 16 * u + tile
                pbank = psC if (gt % 2 == 0) else psD
                v.wait_ge(TS.h, unit_ts[u] + nsl * 4 + tile + 1)
                asl = acc[:, tile * D:(tile + 1) * D]
                if u == 0:
                    if tile == 0:
                        v.wait_ge(QS.h, qs_sh2b)
                    VS.inc(v.tensor_tensor(asl, pbank[:], w_sh2bB[:], ALU.add))
                elif kind == "sq":
                    VS.inc(v.tensor_tensor(asl, pbank[:], asl, ALU.add))
                else:
                    VS.inc(v.scalar_tensor_tensor(
                        asl, pbank[:], gates[:, idx * NT + tile:idx * NT + tile + 1],
                        asl, ALU.mult, ALU.add))
        assert VS.n == vs_b_end


    if "hbuf" in d_dbg or "wf2" in d_dbg:
        @block.gpsimd
        def _(g):
            g.wait_ge(VS.h, vs_b_end)
            if "hbuf" in d_dbg:
                QG.inc(g.dma_start(out=d_dbg["hbuf"][:], in_=hbuf[:]), 16)
            if "wf2" in d_dbg:
                QG.inc(g.dma_start(out=d_dbg["wf2"][:], in_=wf2[:]), 16)
    if stop_after == "B":
        bscope.close()
        accscope.close()
        stack.close()
        return nc, d_dbg

    TCH = 256
    NTC = T // TCH
    NB = TCH // 32
    NM = TCH // 128

    spbuf = wf1
    ss_c0, qs_c0 = SS.n, QS.n
    gspr = d_gsp.rearrange("(n p) d -> n p d", p=128)
    yr = d_y.rearrange("(n p) d -> n p d", p=128)

    @block.scalar
    def _(s):
        for i in range(NT):
            s.wait_ge(VS.h, vs_of_gt[16 * (len(units) - 1) + i])
            if i >= 2:
                s.wait_ge(QS.h, qs_c0 + 16 * (i - 1))
            SS.inc(s.activation(spbuf[:, (i % 2) * D:(i % 2 + 1) * D],
                                acc[:, i * D:(i + 1) * D], AF.Silu))
    assert SS.n == ss_c0 + NT

    @block.sync
    def _(sync):
        for i in range(NT):
            sync.wait_ge(SS.h, ss_c0 + i + 1)
            QS.inc(sync.dma_start(out=gspr[i],
                                  in_=spbuf[:, (i % 2) * D:(i % 2 + 1) * D]), 16)
        if "acc2" in d_dbg:
            sync.wait_ge(VS.h, vs_b_end)
            QS.inc(sync.dma_start(out=d_dbg["acc2"][:], in_=acc[:]), 16)
    qs_spill_end = QS.n
    assert qs_spill_end == qs_c0 + 16 * NT + (16 if "acc2" in d_dbg else 0)

    bscope.close()
    accscope.close()
    if cstop == "spill":
        stack.close()
        return nc, d_dbg

    cscope = ExitStack()
    csb = lambda n, s, dt_, **kw: cscope.enter_context(nc.sbuf_tensor(n, s, dt_, **kw))
    wqkv = csb("wqkv", [128, 2 * 16384], BF16)
    wpT = csb("wpT", [64, 16 * 1024], BF16)
    qBD = csb("qBD", [128, 16 * TCH], BF16)
    kBD = csb("kBD", [128, 16 * TCH], BF16)
    vBDT = csb("vBDT", [128, 16 * TCH], BF16)
    svT = csb("svT", [64, 16 * TCH], BF16)
    expb = csb("expb", [128, 512], F32)
    nmax = csb("nmax", [128, 4], F32)
    sums = csb("sums", [128, 4], F32)
    sums2 = csb("sums2", [128, 4], F32)
    expc = csb("expc", [128, 512], F32)
    c_bdneg = csb("c_bdneg", [128, 128], F32)
    attnb = csb("attnb", [128, 512], BF16)
    attnT = csb("attnT", [128, 512], BF16)
    vBDs = csb("vBDs", [128, 256], BF16)
    gbuf = csb("gbuf", [128, D], BF16)
    xbuf = csb("xbuf", [128, D], F32)
    obuf = csb("obuf", [128, D], F32)
    w_bprojB = csb("w_bprojB", [128, D], F32)
    c_bdmask = csb("c_bdmask", [128, 128], F32)

    qg_c0 = QG.n
    ts_c0, vs_c0, ss_c1 = TS.n, VS.n, SS.n
    TE_PER_TC = 8 + 16 + 16 + NB * 7 + NM
    V_PER_TC = 8 + 16 + 16 + NB * 4 + NM
    S_PER_TC = 2 * NB

    def t_q(tc, hp): return ts_c0 + TE_PER_TC * tc + hp + 1
    def t_k(tc, g): return ts_c0 + TE_PER_TC * tc + 8 + g + 1
    def t_v(tc, g): return ts_c0 + TE_PER_TC * tc + 24 + g + 1
    def t_sc(tc, gb, j): return ts_c0 + TE_PER_TC * tc + 40 + 7 * gb + j + 1
    def t_at(tc, gb): return ts_c0 + TE_PER_TC * tc + 40 + 7 * gb + 5
    def t_vt(tc, gb): return ts_c0 + TE_PER_TC * tc + 40 + 7 * gb + 6
    def t_av(tc, gb): return ts_c0 + TE_PER_TC * tc + 40 + 7 * gb + 7
    def t_op(tc, m): return ts_c0 + TE_PER_TC * tc + 40 + 7 * NB + m + 1

    def v_q(tc, hp): return vs_c0 + V_PER_TC * tc + hp + 1
    def v_k(tc, g): return vs_c0 + V_PER_TC * tc + 8 + g + 1
    def v_v(tc, g): return vs_c0 + V_PER_TC * tc + 24 + g + 1
    def v_nm(tc, gb): return vs_c0 + V_PER_TC * tc + 40 + 4 * gb + 1
    def v_rc(tc, gb): return vs_c0 + V_PER_TC * tc + 40 + 4 * gb + 2
    def v_ae(tc, gb): return vs_c0 + V_PER_TC * tc + 40 + 4 * gb + 3
    def v_sv(tc, gb): return vs_c0 + V_PER_TC * tc + 40 + 4 * gb + 4
    def v_fin(tc, m): return vs_c0 + V_PER_TC * tc + 40 + 4 * NB + m + 1

    def s_exp(tc, gb): return ss_c1 + S_PER_TC * tc + 2 * gb + 1
    def s_no(tc, gb): return ss_c1 + S_PER_TC * tc + 2 * gb + 2

    def qg_w(m): return qg_c0 + 16 + 16 * (m + 1)

    if cstop and cstop.startswith("tc"):
        NTC_E = int(cstop[2:]) + 1
    elif cstop:
        NTC_E = 1
    else:
        NTC_E = NTC

    @block.gpsimd
    def _(g):
        g.wait_ge(SS.h, ss_c0 + NT)
        if "acc2" in d_dbg:
            g.wait_ge(QS.h, qs_spill_end)
        QG.inc(g.dma_start(out=wpT[:], in_=d_wpT[:]), 16)
        for tc in range(NTC_E):
            for kind in range(3):
                m = 3 * tc + kind
                if m >= 2:
                    tc2, kd2 = divmod(m - 2, 3)
                    done = (t_q(tc2, 7) if kd2 == 0 else
                            t_k(tc2, 15) if kd2 == 1 else t_v(tc2, 15))
                    g.wait_ge(TS.h, done)
                sl = (m % 2) * 16384
                if kind == 0:
                    QG.inc(g.dma_start(
                        out=wqkv[:, sl:sl + 8 * 1024].rearrange("p (c n) -> p c n", c=8),
                        in_=d_wqT.rearrange("(c p) n -> p c n", p=128)), 16)
                else:
                    srcw = d_wkTd if kind == 1 else d_wvTd
                    QG.inc(g.dma_start(
                        out=wqkv[:, sl:sl + 16384].rearrange("p (c n) -> p c n", c=8),
                        in_=srcw.rearrange("(c p) n -> p c n", p=128)), 16)
    qg_c_end = QG.n

    qs_load = {}
    qs_store = {}

    @block.sync
    def _(sync):
        QS.inc(sync.dma_start(out=w_bprojB[:], in_=d_bprojB[:]), 16)
        QS.inc(sync.dma_start(out=c_bdmask[:], in_=d_bdmask[:]), 16)
        QS.inc(sync.dma_start(out=c_bdneg[:], in_=d_bdneg[:]), 16)
        n_gmt = (16 if not cstop else
                 (2 * NTC_E if cstop.startswith("tc") else 0))
        for gmt in range(n_gmt):
            if gmt >= 1:
                tcp, mp = divmod(gmt - 1, NM)
                sync.wait_ge(VS.h, v_fin(tcp, mp))
                QS.inc(sync.dma_start(out=yr[gmt - 1], in_=obuf[:]), 16)
                qs_store[gmt - 1] = QS.n
            QS.inc(sync.dma_start(out=gbuf[:], in_=gspr[gmt]), 16)
            QS.inc(sync.dma_start(out=xbuf[:], in_=xLr[gmt]), 16)
            qs_load[gmt] = QS.n
        if not cstop:
            tcp, mp = divmod(15, NM)
            sync.wait_ge(VS.h, v_fin(tcp, mp))
            QS.inc(sync.dma_start(out=yr[15], in_=obuf[:]), 16)
            qs_store[15] = QS.n
        sync.wait_ge(QS.h, QS.n)

    _ord = {"qkv": 1, "scores": 2, "softmax": 3, "attnv": 4, "outproj": 5,
            None: 9}
    LVL = 9 if (cstop and cstop.startswith("tc")) else _ord.get(cstop, 9)
    NB_E = 1 if cstop in ("scores", "softmax", "attnv") else NB

    @block.tensor
    def _(t):
        t.wait_ge(SS.h, ss_b_end)
        for tc in range(NTC_E):
            for hp in range(8):
                gqi = hp
                pbank = psA if gqi % 2 == 0 else psB
                t.wait_ge(QG.h, qg_w(3 * tc + 0))
                if gqi >= 2:
                    t.wait_ge(VS.h, v_q(tc, hp - 2))
                elif tc >= 1:
                    t.wait_ge(VS.h, v_fin(tc - 1, NM - 1))
                sl = ((3 * tc + 0) % 2) * 16384
                for dc in range(8):
                    mm = t.matmul(pbank[:, 0:TCH],
                                  wqkv[:, sl + dc * 1024 + hp * 128:
                                       sl + dc * 1024 + (hp + 1) * 128],
                                  xnT[:, dc * T + tc * TCH:dc * T + (tc + 1) * TCH],
                                  start=(dc == 0), stop=(dc == 7))
                TS.inc(mm)
            for gh in range(16):
                gqi = 8 + gh
                pbank = psA if gqi % 2 == 0 else psB
                t.wait_ge(QG.h, qg_w(3 * tc + 1))
                prev = gqi - 2
                t.wait_ge(VS.h, (v_q(tc, prev) if prev < 8 else v_k(tc, prev - 8)))
                sl = ((3 * tc + 1) % 2) * 16384
                for dc in range(8):
                    mm = t.matmul(pbank[:, 0:TCH],
                                  wqkv[:, sl + dc * 2048 + gh * 128:
                                       sl + dc * 2048 + (gh + 1) * 128],
                                  xnT[:, dc * T + tc * TCH:dc * T + (tc + 1) * TCH],
                                  start=(dc == 0), stop=(dc == 7))
                TS.inc(mm)
            for gh in range(16):
                gqi = 24 + gh
                pbank = psA if gqi % 2 == 0 else psB
                t.wait_ge(QG.h, qg_w(3 * tc + 2))
                prev = gqi - 2
                t.wait_ge(VS.h, (v_k(tc, prev - 8) if prev < 24 else v_v(tc, prev - 24)))
                sl = ((3 * tc + 2) % 2) * 16384
                for dc in range(8):
                    mm = t.matmul(pbank[:, 0:TCH],
                                  wqkv[:, sl + dc * 2048 + gh * 128:
                                       sl + dc * 2048 + (gh + 1) * 128],
                                  xnT[:, dc * T + tc * TCH:dc * T + (tc + 1) * TCH],
                                  start=(dc == 0), stop=(dc == 7))
                TS.inc(mm)
            bl = {0: psC[:, 0:256], 1: psC[:, 512:768],
                  2: psD[:, 0:256], 3: psD[:, 512:768]}
            for gb in range(NB_E if LVL >= 2 else 0):
                t.wait_ge(VS.h, v_k(tc, 15))
                if gb >= 1:
                    t.wait_ge(VS.h, v_nm(tc, gb - 1))
                elif tc >= 1:
                    t.wait_ge(VS.h, v_nm(tc - 1, NB - 1))
                else:
                    t.wait_ge(VS.h, vs_b_end)
                for j in range(4):
                    G = 4 * gb + j
                    mm = t.matmul(bl[j][:, 0:128], qBD[:, G * 128:(G + 1) * 128],
                                  kBD[:, G * 128:(G + 1) * 128], start=True, stop=True)
                    TS.inc(mm)
                if LVL < 3:
                    continue
                t.wait_ge(SS.h, s_no(tc, gb))
                if gb >= 1:
                    t.wait_ge(VS.h, v_ae(tc, gb - 1))
                elif tc >= 1:
                    t.wait_ge(VS.h, v_ae(tc - 1, NB - 1))
                for j in range(4):
                    mm = t.transpose(psT[:, j * 128:(j + 1) * 128],
                                     attnb[:, j * 128:(j + 1) * 128], c_ident[:])
                TS.inc(mm)
                t.wait_ge(VS.h, v_v(tc, 15))
                for j in range(4):
                    G = 4 * gb + j
                    mm = t.transpose(psU[:, j * 128:(j + 1) * 128],
                                     vBDT[:, G * 128:(G + 1) * 128], c_ident[:])
                TS.inc(mm)
                if LVL < 4:
                    continue
                t.wait_ge(VS.h, v_ae(tc, gb))
                for j in range(4):
                    mm = t.matmul(bl[j][0:64, 128:256], vBDs[:, j * 64:(j + 1) * 64],
                                  attnT[:, j * 128:(j + 1) * 128], start=True, stop=True)
                TS.inc(mm)
            if LVL < 5:
                continue
            t.wait_ge(VS.h, v_sv(tc, NB - 1))
            t.wait_ge(QG.h, qg_c0 + 16)
            for m in range(1 if cstop == "outproj" else NM):
                gmt = tc * NM + m
                if gmt >= 1:
                    tcp, mp = divmod(gmt - 1, NM)
                    t.wait_ge(VS.h, v_fin(tcp, mp))
                for h in range(16):
                    lhs = svT[0:64].rearrange("q (t s) -> q t s", s=16)[
                        :, m * 128:(m + 1) * 128, h]
                    for nh in range(2):
                        pbank = psA if nh == 0 else psB
                        mm = t.matmul(pbank[:, 0:512], lhs,
                                      wpT[0:64, h * 1024 + nh * 512:
                                          h * 1024 + (nh + 1) * 512],
                                      start=(h == 0), stop=(h == 15))
                TS.inc(mm)
        if not cstop:
            assert TS.n == ts_c0 + NTC * TE_PER_TC

    @block.scalar
    def _(s):
        bl = {0: psC[:, 0:256], 1: psC[:, 512:768],
              2: psD[:, 0:256], 3: psD[:, 512:768]}
        for tc in range(NTC_E):
            for gb in range(NB_E if LVL >= 3 else 0):
                s.wait_ge(VS.h, v_nm(tc, gb))
                for j in range(4):
                    s.activation(expc[:, j * 128:(j + 1) * 128],
                                 expb[:, j * 128:(j + 1) * 128],
                                 AF.Exp, bias=nmax[:, j:j + 1],
                                 accum_out=sums[:, j:j + 1])
                SS.inc(s.drain())
                s.wait_ge(VS.h, v_rc(tc, gb))
                for j in range(4):
                    no = s.activation(attnb[:, j * 128:(j + 1) * 128],
                                      expc[:, j * 128:(j + 1) * 128],
                                      AF.Identity, scale=sums2[:, j:j + 1])
                SS.inc(no)
        if not cstop:
            assert SS.n == ss_c1 + NTC * S_PER_TC

    @block.vector
    def _(v):
        bl = {0: psC[:, 0:256], 1: psC[:, 512:768],
              2: psD[:, 0:256], 3: psD[:, 512:768]}
        v.memset(qBD[:], 0.0)
        v.drain()
        for tc in range(NTC_E):
            if tc >= 1:
                v.wait_ge(TS.h, t_sc(tc - 1, NB - 1, 3))
            for hp in range(8):
                pbank = psA if hp % 2 == 0 else psB
                v.wait_ge(TS.h, t_q(tc, hp))
                dst_lo = qBD[0:64].rearrange("q (t s) -> q t s", s=16)[:, :, 2 * hp]
                dst_hi = qBD[64:128].rearrange("q (t s) -> q t s", s=16)[:, :, 2 * hp + 1]
                v.tensor_scalar(dst_lo, pbank[0:64, 0:TCH], 0.125, None, ALU.mult)
                VS.inc(v.tensor_scalar(dst_hi, pbank[64:128, 0:TCH], 0.125, None,
                                       ALU.mult))
            for gh in range(16):
                pbank = psA if (8 + gh) % 2 == 0 else psB
                v.wait_ge(TS.h, t_k(tc, gh))
                dst = kBD[:].rearrange("q (t s) -> q t s", s=16)[:, :, gh]
                VS.inc(v.tensor_copy(dst, pbank[:, 0:TCH]))
            for gh in range(16):
                pbank = psA if (24 + gh) % 2 == 0 else psB
                v.wait_ge(TS.h, t_v(tc, gh))
                dst = vBDT[:].rearrange("q (t s) -> q t s", s=16)[:, :, gh]
                VS.inc(v.tensor_copy(dst, pbank[:, 0:TCH]))
            for gb in range(NB_E if LVL >= 2 else 0):
                v.wait_ge(TS.h, t_sc(tc, gb, 3))
                if tc == 0 and gb == 0:
                    v.wait_ge(QS.h, qs_spill_end + 48)
                for j in range(4):
                    v.tensor_tensor(expb[:, j * 128:(j + 1) * 128],
                                    bl[j][:, 0:128], c_bdmask[:], ALU.mult)
                    v.tensor_tensor(expb[:, j * 128:(j + 1) * 128],
                                    expb[:, j * 128:(j + 1) * 128],
                                    c_bdneg[:], ALU.add)
                for j in range(4):
                    nm = v.tensor_reduce(nmax[:, j:j + 1],
                                         expb[:, j * 128:(j + 1) * 128], AX.X,
                                         ALU.max, negate=True)
                VS.inc(nm)
                if LVL < 3:
                    continue
                v.wait_ge(SS.h, s_exp(tc, gb))
                VS.inc(v.reciprocal(sums2[:], sums[:]))
                v.wait_ge(TS.h, t_vt(tc, gb))
                for j in range(4):
                    v.tensor_copy(attnT[:, j * 128:(j + 1) * 128],
                                  psT[:, j * 128:(j + 1) * 128])
                for j in range(4):
                    ae = v.tensor_copy(vBDs[:, j * 64:(j + 1) * 64],
                                       psU[:, j * 128:j * 128 + 64])
                VS.inc(ae)
                if LVL < 4:
                    continue
                v.wait_ge(TS.h, t_av(tc, gb))
                for j in range(4):
                    G = 4 * gb + j
                    sv = v.tensor_copy(svT[0:64, G * 128:(G + 1) * 128],
                                       bl[j][0:64, 128:256])
                VS.inc(sv)
            for m in range(NM if LVL >= 5 and
                           (cstop is None or cstop.startswith("tc")) else 0):
                gmt = tc * NM + m
                v.wait_ge(TS.h, t_op(tc, m))
                v.wait_ge(QS.h, qs_load[gmt])
                if gmt >= 1:
                    v.wait_ge(QS.h, qs_store[gmt - 1])
                v.tensor_tensor(obuf[:, 0:512], psA[:, 0:512],
                                w_bprojB[:, 0:512], ALU.add)
                v.tensor_tensor(obuf[:, 512:1024], psB[:, 0:512],
                                w_bprojB[:, 512:1024], ALU.add)
                v.tensor_tensor(obuf[:], obuf[:], gbuf[:], ALU.mult)
                VS.inc(v.tensor_tensor(obuf[:], obuf[:], xbuf[:], ALU.add))
        if not cstop:
            assert VS.n == vs_c0 + NTC * V_PER_TC

    cscope.close()
    stack.close()
    return nc, d_dbg


def _prep_inputs(L0_attn_x, L0_attn_Wqkv, L0_attn_Wproj, L0_attn_bproj,
                 L0_ffn_proj_w, L0_ffn_proj_b, L0_ffn_expert_bias,
                 L0_ffn_fc1_w, L0_ffn_fc1_b, L0_ffn_fc2_w, L0_ffn_fc2_b,
                 L0_ffn_sh1_w, L0_ffn_sh1_b, L0_ffn_sh2_w, L0_ffn_sh2_b):
    f32 = np.float32
    x2d = np.ascontiguousarray(np.asarray(L0_attn_x, f32).reshape(NTOK, D))
    xT = np.ascontiguousarray(x2d.T)
    Wf = np.asarray(L0_attn_Wqkv, f32).reshape(16, 192, D)
    WqT = np.ascontiguousarray(Wf[:, 0:64].reshape(1024, D).T)
    WkT = np.ascontiguousarray(Wf[:, 64:128].reshape(1024, D).T)
    WvT = np.ascontiguousarray(Wf[:, 128:192].reshape(1024, D).T)

    def dup(WT):
        out = np.empty((D, 2048), f32)
        for h in range(16):
            out[:, 128 * h:128 * h + 64] = WT[:, 64 * h:64 * h + 64]
            out[:, 128 * h + 64:128 * h + 128] = WT[:, 64 * h:64 * h + 64]
        return np.ascontiguousarray(out)

    wkTd = dup(WkT)
    wvTd = dup(WvT)
    WpT = np.asarray(L0_attn_Wproj, f32).T
    wpT64 = np.ascontiguousarray(
        WpT.reshape(16, 64, D).transpose(1, 0, 2).reshape(64, 16 * D))
    bprojB = np.ascontiguousarray(
        np.broadcast_to(np.asarray(L0_attn_bproj, f32), (128, D)))
    fc1T = np.ascontiguousarray(np.asarray(L0_ffn_fc1_w, f32).transpose(0, 2, 1))
    fc2T = np.ascontiguousarray(np.asarray(L0_ffn_fc2_w, f32).transpose(0, 2, 1))
    sh1T = np.ascontiguousarray(np.asarray(L0_ffn_sh1_w, f32).T)
    sh2T = np.ascontiguousarray(np.asarray(L0_ffn_sh2_w, f32).T)
    fc1bP = np.ascontiguousarray(
        np.asarray(L0_ffn_fc1_b, f32).reshape(E, 16, 128)
        .transpose(2, 0, 1).reshape(128, E * 16))
    sh1bP = np.ascontiguousarray(np.asarray(L0_ffn_sh1_b, f32).reshape(32, 128).T)
    fc2bA = np.ascontiguousarray(np.asarray(L0_ffn_fc2_b, f32))
    sh2bB = np.ascontiguousarray(
        np.broadcast_to(np.asarray(L0_ffn_sh2_b, f32), (128, D)))
    pw9 = np.ascontiguousarray(
        np.concatenate([np.ones((D, 1), f32), np.asarray(L0_ffn_proj_w, f32).T], 1))
    ebP = np.ascontiguousarray(
        np.broadcast_to(np.asarray(L0_ffn_expert_bias, f32), (128, E)))
    pbP = np.ascontiguousarray(
        np.broadcast_to(np.asarray(L0_ffn_proj_b, f32), (128, E)))

    cones = np.ones((128, 128), f32)
    clt = np.ascontiguousarray(np.tril(np.ones((128, 128), f32), k=-1).T)
    cident = np.eye(128, dtype=f32)
    cbdmask = np.zeros((128, 128), f32)
    for jj in range(8):
        cbdmask[16 * jj:16 * jj + 16, 16 * jj:16 * jj + 16] = 1.0
    ciotaE = np.ascontiguousarray(
        np.repeat(np.arange(E, dtype=f32), NR)[None, :].repeat(128, 0))
    ce8 = np.zeros((8, E * 128), f32)
    for e in range(E):
        ce8[e, e * 128:(e + 1) * 128] = 1.0
    cio16 = np.broadcast_to((np.arange(NR) // NT).astype(f32), (128, NR)).copy()
    bisc = np.broadcast_to(np.array([2.0 ** (-k) for k in range(40)] + [float(CAP)],
                                    f32), (128, 41)).copy()
    epsB = np.full((128, 1), 1e-5, f32)

    shared = dict(xT=xT, wqT=WqT, wkTd=wkTd, wvTd=wvTd, wpT64=wpT64,
                  bprojB=bprojB, fc1T=fc1T, fc2T=fc2T, sh1T=sh1T, sh2T=sh2T,
                  fc1bP=fc1bP, sh1bP=sh1bP, fc2bA=fc2bA, sh2bB=sh2bB,
                  pw9=pw9, ebP=ebP, pbP=pbP, cones=cones, clt=clt,
                  cident=cident, cbdmask=cbdmask,
                  cbdneg=(cbdmask - 1.0) * 30000.0, ciotaE=ciotaE, ce8=ce8,
                  cio16=cio16, bisc=bisc, epsB=epsB)
    in_maps = []
    for c in range(NCORE):
        m = dict(shared)
        m["xL"] = np.ascontiguousarray(x2d[c * T:(c + 1) * T])
        m["cid"] = np.full((128, 1), float(c), f32)
        in_maps.append(m)
    return in_maps


def kernel(**inputs):
    in_maps = _prep_inputs(**inputs)
    nc, _ = build_program()
    res = run_bass_kernel_spmd(nc, in_maps, core_ids=list(range(NCORE)))
    out = np.concatenate([np.asarray(r["y"]) for r in res.results], axis=0)
    return np.ascontiguousarray(out.reshape(4, 4096, D).astype(np.float32))
